# version 04
# baseline (speedup 1.0000x reference)
import contextlib
import numpy as np
import ml_dtypes
import concourse.bacc as bacc
import concourse.bass as bass
import concourse.mybir as mybir
from concourse.bass_utils import run_bass_kernel_spmd

F32 = mybir.dt.float32
BF16 = mybir.dt.bfloat16
I32 = mybir.dt.int32
AF = mybir.ActivationFunctionType
ALU = mybir.AluOpType
BF16NP = ml_dtypes.bfloat16

D = 2048
NB = 16
T = 8192
NCORE = 8
CH = 512
TOWN = 1024
SUB = 128
RH = 32
RN = 64
AH = 16
AHD = 128
FM = 8192
FB = 64
DMIX = 32
DDEC = 64
EPS_LN = 1e-5
EPS_GN = 1e-5 * 64.0
EPS_RMS = 1e-6
MOD_SCALE = 0.1

SLAB = 1056
GROUPS = ((0, 512), (512, 1024), (1024, 1056))

ENGINES = ("sync", "tensor", "vector", "scalar", "gpsimd")

ARENA_COLS = 47616

VEC_NAMES = ["maa_x", "maa_w", "maa_k", "maa_v", "maa_r", "maa_g",
             "time_decay", "u_faaaa", "ln_gamma", "ln_beta",
             "ffn_ln1", "ffn_ln2", "c_raw"]


def _owner(c):
    return c if c <= 7 else 15 - c


class Prog:
    def __init__(self, nc, es):
        self.nc = nc
        self.es = es
        self.q = {e: [] for e in ENGINES}
        self.sems = {}
        self.cnt = {}

    def sem(self, name):
        if name not in self.sems:
            self.sems[name] = self.es.enter_context(self.nc.semaphore(name))
            self.cnt[name] = 0
        return self.sems[name]

    def emit(self, eng, fn):
        self.q[eng].append(fn)

    def op(self, eng, fn, inc=None, inc_n=1):
        if inc is not None:
            s = self.sem(inc)
            self.cnt[inc] += inc_n
            self.q[eng].append(lambda E, fn=fn, s=s, n=inc_n: fn(E).then_inc(s, n))
        else:
            self.q[eng].append(lambda E, fn=fn: fn(E))

    def dma(self, eng, out, in_, sem, **kw):
        s = self.sem(sem)
        self.cnt[sem] += 16
        self.q[eng].append(
            lambda E, o=out, i=in_, s=s, kw=kw: E.dma_start(out=o, in_=i, **kw).then_inc(s, 16))

    def wait(self, eng, sem, val=None):
        s = self.sem(sem)
        v = self.cnt[sem] if val is None else val
        if v > 0:
            self.q[eng].append(lambda E, s=s, v=v: E.wait_ge(s, v))

    def mark(self, sem):
        self.sem(sem)
        return self.cnt[sem]

    def drain(self, eng):
        self.q[eng].append(lambda E: E.drain())

    def barrier(self, engines=ENGINES):
        vals = {}
        for e in engines:
            name = "bar_" + e
            s = self.sem(name)
            self.cnt[name] += 1
            vals[e] = self.cnt[name]
            self.q[e].append(lambda E, s=s: E.drain().then_inc(s, 1))
        for e in engines:
            for o in engines:
                if o == e:
                    continue
                s = self.sems["bar_" + o]
                self.q[e].append(lambda E, s=s, v=vals[o]: E.wait_ge(s, v))

    def install(self):
        block = self.es.enter_context(self.nc.Block())
        prog = self

        def mk(name):
            def body(E):
                for fn in prog.q[name]:
                    fn(E)
            body.__name__ = "_" + name
            return body
        block.sync(mk("sync"))
        block.tensor(mk("tensor"))
        block.vector(mk("vector"))
        block.scalar(mk("scalar"))
        block.gpsimd(mk("gpsimd"))


class Arena:

    def __init__(self, tensor, cols):
        self.t = tensor
        self.cols = cols
        self.top = 0
        self.stack = []
        self.hwm = 0
        self.log = []

    def alloc(self, cols_f32, name=""):
        cols_f32 = (cols_f32 + 7) & ~7
        assert self.top + cols_f32 <= self.cols, (
            f"SBUF arena overflow: {name} needs {cols_f32} cols, "
            f"{self.cols - self.top} free")
        ap = self.t[:, self.top:self.top + cols_f32]
        self.log.append((name, self.top, cols_f32))
        self.top += cols_f32
        self.hwm = max(self.hwm, self.top)
        return ap

    def alloc_bf16(self, cols_bf16, name=""):
        return self.alloc((cols_bf16 + 1) // 2, name).bitcast(BF16)

    def push(self):
        self.stack.append(self.top)

    def pop(self):
        self.top = self.stack.pop()


class Builder:
    def __init__(self, dbg=(), wkv_parts=3):
        self.nc = bacc.Bacc("TRN2", target_bir_lowering=False)
        self.dbg = set(dbg)
        self.wkv_parts = wkv_parts
        self.wkv_sub = 6
        self.wkv_fine = 0xff
        self.dbg_outs = {}
        self.din = {}

    def dram_in(self, name, shape, dtype=F32):
        t = self.nc.dram_tensor(name, list(shape), dtype, kind="ExternalInput")
        self.din[name] = t
        return t

    def dram_tmp(self, name, shape, dtype=F32, shared=False):
        if shared:
            return self.nc.dram_tensor(name, list(shape), dtype, addr_space="Shared")
        return self.nc.dram_tensor(name, list(shape), dtype)

    def tap(self, name, ap_fn, shape, dtype=F32):
        if name not in self.dbg:
            return
        t = self.nc.dram_tensor("dbg_" + name, list(shape), dtype, kind="ExternalOutput")
        self.dbg_outs[name] = t
        P = self.P
        P.barrier()
        P.dma("gpsimd", t[:], ap_fn(), "dout")
        P.wait("gpsimd", "dout")

    def vec(self, name):
        i = VEC_NAMES.index(name)
        return self.sb_vecs[:, i * NB:(i + 1) * NB]

    def modv(self, v):
        return self.sb_mod[:, v * NB:(v + 1) * NB]

    def build(self):
        nc = self.nc
        self.dram_in("xslab", [D, SLAB])
        for n in ("wr", "wk", "wv", "wg", "wo", "wq_a", "wk_a", "wv_a", "wo_a"):
            self.dram_in(n, [D, D])
        self.dram_in("w_in", [D, FM])
        self.dram_in("w_up", [D, FM])
        self.dram_in("w_down", [FM, D])
        self.dram_in("maa_w1", [D, 5 * DMIX])
        self.dram_in("maa_w2", [5 * DMIX, D])
        self.dram_in("dec_w1", [D, DDEC])
        self.dram_in("dec_w2", [DDEC, D])
        self.dram_in("a_lora", [D, AH])
        self.dram_in("b_lora", [AH, FM])
        self.dram_in("wmod_sl", [D, 6 * D // NCORE])
        self.dram_in("vecs", [128, NB * len(VEC_NAMES)])
        self.dram_in("ident", [128, 128])
        self.dram_in("wkv_mask", [128, 128])
        self.dram_in("ind2", [128, 2])
        self.dram_in("ind2t", [2, 128])
        self.dram_in("att_masks", [128, 6 * 512])
        self.dram_in("sel_wkv", [128, 64])
        self.dram_in("att_qsel", [128, 18])
        self.dram_in("att_tab", [128, 2 * 16 * 2])
        self.dram_in("att_sched", [1, 128], I32)
        self.dram_in("bndflag", [128, 2])
        self.out_xT = nc.dram_tensor("out_xT", [D, TOWN], F32, kind="ExternalOutput")

        self.mod_bounce = self.dram_tmp("mod_bounce", [1, 6 * D // NCORE])
        self.mod_gat = self.dram_tmp("mod_gat", [NCORE, 6 * D // NCORE], shared=True)
        self.sp_h = self.dram_tmp("sp_h", [D, SLAB])
        self.sp_r = self.dram_tmp("sp_r", [D, TOWN])
        self.sp_k = self.dram_tmp("sp_k", [D, TOWN])
        self.sp_g = self.dram_tmp("sp_g", [D, TOWN])
        self.sp_c = self.dram_tmp("sp_c", [D, TOWN])
        self.sp_v = self.dram_tmp("sp_v", [TOWN, D])
        self.sp_y = self.dram_tmp("sp_y", [D, TOWN])
        self.sp_rth = self.dram_tmp("sp_rth", [D, TOWN], BF16)
        self.sp_rtl = self.dram_tmp("sp_rtl", [D, TOWN], BF16)
        self.sp_x1 = self.dram_tmp("sp_x1", [D, TOWN])
        self.wkv_bounce = self.dram_tmp("wkv_bounce", [2, 128, 1040])
        self.wkv_gat = self.dram_tmp("wkv_gat", [16, 128, 1040], shared=True)
        self.sp_dbg = self.dram_tmp("sp_dbg", [D, TOWN])
        self.sp_h2 = self.dram_tmp("sp_h2", [D, TOWN])
        self.sp_qT = self.dram_tmp("sp_qT", [D, TOWN])
        self.sp_hmh = self.dram_tmp("sp_hmh", [D, TOWN], BF16)
        self.sp_hml = self.dram_tmp("sp_hml", [D, TOWN], BF16)
        self.sp_att = self.dram_tmp("sp_att", [D, TOWN])
        self.kv_bounce = self.dram_tmp("kv_bounce", [4096 + 1, 1024])
        self.kv_gat = self.dram_tmp("kv_gat", [NCORE * (4096 + 1), 1024], shared=True)
        self.sp_lor = self.dram_tmp("sp_lor", [16, TOWN])

        es = contextlib.ExitStack()
        with es:
            self.P = P = Prog(nc, es)
            self.AR = Arena(es.enter_context(
                nc.sbuf_tensor("arena", [128, ARENA_COLS], F32)), ARENA_COLS)
            self.sb_vecs = es.enter_context(
                nc.sbuf_tensor("sb_vecs", [128, NB * len(VEC_NAMES)], F32))
            self.sb_ident = es.enter_context(nc.sbuf_tensor("sb_ident", [128, 128], F32))
            self.sb_wkvmask = es.enter_context(nc.sbuf_tensor("sb_wkvmask", [128, 128], F32))
            self.sb_ind2 = es.enter_context(nc.sbuf_tensor("sb_ind2", [128, 2], F32))
            self.sb_ind2t = es.enter_context(nc.sbuf_tensor("sb_ind2t", [2, 128], F32))
            self.sb_amasks = es.enter_context(nc.sbuf_tensor("sb_amasks", [128, 6 * 512], F32))
            self.sb_selw = es.enter_context(nc.sbuf_tensor("sb_selw", [128, 64], F32))
            self.sb_qsel = es.enter_context(nc.sbuf_tensor("sb_qsel", [128, 18], F32))
            self.sb_atab = es.enter_context(nc.sbuf_tensor("sb_atab", [128, 64], F32))
            self.sb_sched = es.enter_context(nc.sbuf_tensor("sb_sched", [1, 128], I32))
            self.sb_bnd = es.enter_context(nc.sbuf_tensor("sb_bnd", [128, 2], F32))
            self.sb_mod = es.enter_context(nc.sbuf_tensor("sb_mod", [128, 6 * NB], F32))
            self.sb_ones = es.enter_context(nc.sbuf_tensor("sb_ones", [128, 128], F32))
            self.sb_cst = es.enter_context(nc.sbuf_tensor("sb_cst", [128, 8], F32))
            self.psum = [es.enter_context(nc.psum_tensor(f"pb{i}", [128, 512], F32))
                         for i in range(8)]

            self.stage_consts()
            self.stage_mod()
            self.stage_adaln()
            self.stage_rwkv_pre()
            if self.wkv_parts >= 1:
                self.stage_wkv()
            if self.wkv_parts >= 4:
                self.stage_ffn_pre()
                self.stage_attention()
                self.stage_attwo()
                self.stage_mlp()
            self.emit_tail()
            P.barrier()
            P.install()
        return nc

    def stage_consts(self):
        P, din = self.P, self.din
        P.dma("sync", self.sb_vecs[:], din["vecs"][:], "dcst")
        P.dma("sync", self.sb_ident[:], din["ident"][:], "dcst")
        P.dma("sync", self.sb_wkvmask[:], din["wkv_mask"][:], "dcst")
        P.dma("sync", self.sb_ind2[:], din["ind2"][:], "dcst")
        P.dma("sync", self.sb_ind2t[:], din["ind2t"][:], "dcst")
        P.dma("sync", self.sb_amasks[:], din["att_masks"][:], "dcst")
        P.dma("sync", self.sb_selw[:], din["sel_wkv"][:], "dcst")
        P.dma("sync", self.sb_qsel[:], din["att_qsel"][:], "dcst")
        P.dma("sync", self.sb_atab[:], din["att_tab"][:], "dcst")
        P.dma("sync", self.sb_sched[:], din["att_sched"][:], "dcst")
        P.dma("sync", self.sb_bnd[:], din["bndflag"][:], "dcst")
        P.op("vector", lambda E: E.memset(self.sb_ones[:], 1.0))
        P.op("vector", lambda E: E.memset(self.sb_cst[:, 0:1], EPS_LN))
        P.op("vector", lambda E: E.memset(self.sb_cst[:, 1:2], EPS_GN))
        P.op("vector", lambda E: E.memset(self.sb_cst[:, 2:3], EPS_RMS), inc="ve")

    def stage_mod(self):
        P, AR, din, psum = self.P, self.AR, self.din, self.psum
        AR.push()
        SL = 6 * D // NCORE
        wbuf = AR.alloc(NB * SL, "wmod_buf")
        scT = AR.alloc(NB, "scT")
        msl = AR.alloc(SL, "mod_slice")
        for b in range(NB):
            P.dma("sync", wbuf[:, b * SL:(b + 1) * SL],
                  din["wmod_sl"][b * 128:(b + 1) * 128, :], "dwm")
        P.wait("scalar", "dcst")
        P.op("scalar", lambda E: E.activation(out=scT, in_=self.vec("c_raw"),
                                              func=AF.Silu), inc="ae")
        P.wait("tensor", "dwm")
        P.wait("tensor", "dcst")
        P.wait("tensor", "ae")
        for j in range(SL // 512):
            if j > 0:
                P.wait("tensor", "ve")
            for b in range(NB):
                P.op("tensor", lambda E, b=b, j=j: E.matmul(
                    psum[0][0:1, :], scT[:, b:b + 1],
                    wbuf[:, b * SL + j * 512:b * SL + j * 512 + 512],
                    start=(b == 0), stop=(b == NB - 1)),
                    inc=("pe" if b == NB - 1 else None))
            P.wait("vector", "pe")
            P.op("vector", lambda E, j=j: E.tensor_copy(
                out=msl[0:1, j * 512:(j + 1) * 512], in_=psum[0][0:1, :]), inc="ve")
        P.wait("gpsimd", "ve")
        P.dma("gpsimd", self.mod_bounce[:], msl[0:1, 0:SL], "dx")
        P.wait("gpsimd", "dx")
        P.op("gpsimd", lambda E: E.collective_compute(
            "AllGather", ALU.bypass, replica_groups=[list(range(NCORE))],
            ins=[self.mod_bounce.ap()], outs=[self.mod_gat.ap()]), inc="cc")
        P.wait("gpsimd", "cc")
        gat_flat = self.mod_gat.ap().rearrange("a b -> (a b)")
        for v in range(6):
            seg = gat_flat[v * D:(v + 1) * D].rearrange("(a b) -> b a", b=128)
            P.dma("gpsimd", self.sb_mod[:, v * NB:(v + 1) * NB], seg, "dx",
                  allow_slow_non_contiguous=True)
        P.wait("vector", "dx")
        P.wait("scalar", "dx")
        for v in (1, 4):
            m = self.modv(v)
            P.op("vector", lambda E, m=m: E.tensor_scalar_add(out=m, in0=m, scalar1=1.0),
                 inc="ve")
        gtmp = AR.alloc(6 * NB, "gtmp")
        for k, v in enumerate((2, 5)):
            m = self.modv(v)
            t_sign = gtmp[:, (3 * k) * NB:(3 * k + 1) * NB]
            t_abs = gtmp[:, (3 * k + 1) * NB:(3 * k + 2) * NB]
            t_max = gtmp[:, (3 * k + 2) * NB:(3 * k + 3) * NB]
            P.op("vector", lambda E, m=m, t=t_sign: E.tensor_scalar(
                out=t, in0=m, scalar1=0.0, scalar2=None, op0=ALU.is_ge))
            P.drain("vector")
            P.op("vector", lambda E, t=t_sign: E.tensor_scalar(
                out=t, in0=t, scalar1=2.0, scalar2=-1.0, op0=ALU.mult, op1=ALU.add))
            P.op("scalar", lambda E, m=m, t=t_abs: E.activation(out=t, in_=m,
                                                                func=AF.Abs), inc="ae")
            P.drain("vector")
            P.wait("vector", "ae")
            P.op("vector", lambda E, a=t_abs, o=t_max: E.tensor_scalar_max(
                out=o, in0=a, scalar1=0.25), inc="ve")
            P.drain("vector")
            P.op("vector", lambda E, m=m, a=t_max, b=t_sign: E.tensor_tensor(
                out=m, in0=a, in1=b, op=ALU.mult), inc="ve")
        P.barrier()
        self.tap("gdbg", lambda: gtmp, [128, 6 * NB])
        AR.pop()
        self.tap("mod", lambda: self.sb_mod[:], [128, 6 * NB])

    def stage_adaln(self, src=None, ca1v=1, sav=0, tapname="h"):
        P, AR, din, psum = self.P, self.AR, self.din, self.psum
        AR.push()
        h = AR.alloc(NB * SLAB, "h")
        if src is None:
            xb = AR.alloc(NB * SLAB, "x")
            for b in range(NB):
                P.dma("sync", xb[:, b * SLAB:(b + 1) * SLAB],
                      din["xslab"][b * 128:(b + 1) * 128, :], "dxs")
        else:
            xb = src
        sq = AR.alloc(2 * 512, "sq")
        strow = AR.alloc(4 * 512, "stats")
        mu = strow[0:1, 0:512]
        rstd = strow[0:1, 512:1024]
        sc1 = strow[0:1, 1024:1536]
        P.wait("tensor", "dxs")
        P.wait("tensor", "dcst")
        P.wait("tensor", "ve", 1)
        P.wait("scalar", "dxs")
        ca1 = self.modv(ca1v)
        sa = self.modv(sav)
        nsq = 0
        for (cs, ce) in GROUPS:
            W = ce - cs
            for b in range(NB):
                xblk = xb[:, b * SLAB + cs:b * SLAB + ce]
                nsq += 1
                if nsq > 2:
                    P.wait("scalar", "pesq", nsq - 2)
                P.op("scalar", lambda E, x=xblk, b=b, W=W: E.activation(
                    out=sq[:, (b % 2) * 512:(b % 2) * 512 + W], in_=x, func=AF.Square),
                    inc="ae")
                P.op("tensor", lambda E, x=xblk, b=b, W=W: E.matmul(
                    psum[0][0:1, 0:W], self.sb_ones[:, 0:1], x,
                    start=(b == 0), stop=(b == NB - 1)))
                P.wait("tensor", "ae")
                P.op("tensor", lambda E, b=b, W=W: E.matmul(
                    psum[1][0:1, 0:W], self.sb_ones[:, 0:1],
                    sq[:, (b % 2) * 512:(b % 2) * 512 + W],
                    start=(b == 0), stop=(b == NB - 1)),
                    inc="pesq")
            P.wait("vector", "pesq")
            P.op("vector", lambda E, W=W: E.tensor_scalar_mul(
                out=mu[:, 0:W], in0=psum[0][0:1, 0:W], scalar1=1.0 / D), inc="ve")
            P.drain("vector")
            P.op("vector", lambda E, W=W: E.tensor_tensor(
                out=sc1[:, 0:W], in0=mu[:, 0:W], in1=mu[:, 0:W], op=ALU.mult))
            P.drain("vector")
            P.op("vector", lambda E, W=W: E.scalar_tensor_tensor(
                out=rstd[:, 0:W], in0=psum[1][0:1, 0:W], scalar=1.0 / D,
                in1=sc1[:, 0:W], op0=ALU.mult, op1=ALU.subtract), inc="ve")
            ve_stats = P.mark("ve")
            P.wait("scalar", "ve", ve_stats)
            P.op("scalar", lambda E, W=W: E.activation(
                out=rstd[:, 0:W], in_=rstd[:, 0:W], func=AF.Sqrt,
                bias=self.sb_cst[0:1, 0:1]),
                inc="ae")
            P.wait("vector", "ae")
            P.op("vector", lambda E, W=W: E.reciprocal(
                out=rstd[:, 0:W], in_=rstd[:, 0:W]), inc="ve")
            P.wait("tensor", "ve")
            P.op("tensor", lambda E, W=W: E.matmul(
                psum[2][:, 0:W], self.sb_ones[0:1, :], mu[:, 0:W],
                start=True, stop=True))
            P.op("tensor", lambda E, W=W: E.matmul(
                psum[3][:, 0:W], self.sb_ones[0:1, :], rstd[:, 0:W],
                start=True, stop=True), inc="pe")
            P.wait("vector", "pe")
            for b in range(NB):
                xblk = xb[:, b * SLAB + cs:b * SLAB + ce]
                hblk = h[:, b * SLAB + cs:b * SLAB + ce]
                P.op("vector", lambda E, x=xblk, hb=hblk, W=W: E.tensor_tensor(
                    out=hb, in0=x, in1=psum[2][:, 0:W], op=ALU.subtract))
            P.drain("vector")
            for b in range(NB):
                hblk = h[:, b * SLAB + cs:b * SLAB + ce]
                P.op("vector", lambda E, hb=hblk, W=W: E.tensor_tensor(
                    out=hb, in0=hb, in1=psum[3][:, 0:W], op=ALU.mult))
            P.drain("vector")
            for b in range(NB):
                hblk = h[:, b * SLAB + cs:b * SLAB + ce]
                P.op("vector", lambda E, hb=hblk, b=b: E.tensor_scalar(
                    out=hb, in0=hb, scalar1=ca1[:, b:b + 1], scalar2=sa[:, b:b + 1],
                    op0=ALU.mult, op1=ALU.add),
                    inc=("ve" if b == NB - 1 else None))
            P.wait("tensor", "ve")
        P.drain("vector")
        h3 = h.rearrange("p (b c) -> p b c", b=NB)
        bnd = self.sb_bnd[:, 0:2].unsqueeze(1).broadcast_to([128, NB, 2])
        P.op("vector", lambda E: E.tensor_tensor(
            out=h3[:, :, 1024:1026], in0=h3[:, :, 1024:1026], in1=bnd, op=ALU.mult),
            inc="ve")
        for b in range(NB):
            P.wait("gpsimd", "ve")
            P.dma("gpsimd", self.sp_h[b * 128:(b + 1) * 128, :],
                  h[:, b * SLAB:(b + 1) * SLAB], "dxh")
        P.wait("gpsimd", "dxh")
        P.barrier()
        AR.pop()
        self.tap("h", lambda: self.sp_h.ap(), [D, SLAB])

    def split_pair(self, src_f32, dst_hi, dst_lo, ve="ve", ae="ae"):
        P = self.P
        P.op("scalar", lambda E: E.activation(out=dst_hi, in_=src_f32,
                                              func=AF.Identity), inc=ae)
        P.wait("vector", ae)
        P.op("vector", lambda E: E.tensor_tensor(out=dst_lo, in0=src_f32, in1=dst_hi,
                                                 op=ALU.subtract), inc=ve)

    def stage_rwkv_pre(self):
        P, AR, din, psum = self.P, self.AR, self.din, self.psum
        AR.push()
        w2sb1 = AR.alloc(D, "w2sb1")
        w2sb2_f = AR.alloc(D, "w2sb2")
        w2sb2 = w2sb2_f[0:32, :]
        maa1h = AR.alloc_bf16(NB * 160, "maa1h")
        maa1l = AR.alloc_bf16(NB * 160, "maa1l")
        d1h = AR.alloc_bf16(NB * DDEC, "d1h")
        d1l = AR.alloc_bf16(NB * DDEC, "d1l")
        d2h_f = AR.alloc(D // 2, "d2h")
        d2l_f = AR.alloc(D // 2, "d2l")
        d2h = d2h_f.bitcast(BF16)[0:64, :]
        d2l = d2l_f.bitcast(BF16)[0:64, :]
        AR.push()
        tmpw = AR.alloc(NB * 160, "tmpw")
        P.dma("sync", w2sb1[:], din["maa_w2"][0:128, :], "dws")
        P.dma("sync", w2sb2, din["maa_w2"][128:160, :], "dws")
        for b in range(NB):
            P.dma("sync", tmpw[:, b * 160:(b + 1) * 160],
                  din["maa_w1"][b * 128:(b + 1) * 128, :], "dws")
        P.wait("scalar", "dws")
        P.wait("vector", "dws")
        self.split_pair(tmpw, maa1h, maa1l)
        tmpd = AR.alloc(NB * DDEC, "tmpd")
        P.wait("sync", "dws")
        for b in range(NB):
            P.dma("sync", tmpd[:, b * DDEC:(b + 1) * DDEC],
                  din["dec_w1"][b * 128:(b + 1) * 128, :], "dws")
        tmpd2 = AR.alloc(D, "tmpd2")
        P.dma("sync", tmpd2[0:64, :], din["dec_w2"][:], "dws")
        P.wait("scalar", "dws")
        P.wait("vector", "dws")
        self.split_pair(tmpd, d1h, d1l)
        self.split_pair(tmpd2[0:64, :], d2h, d2l)
        P.barrier()
        AR.pop()
        for ch in range(2):
            self._rwkv_pre_chunk(ch, w2sb1, w2sb2, maa1h, maa1l,
                                 d1h, d1l, d2h, d2l)
        AR.pop()
        self.tap("spr", lambda: self.sp_r.ap(), [D, TOWN])
        self.tap("spk", lambda: self.sp_k.ap(), [D, TOWN])
        self.tap("spg", lambda: self.sp_g.ap(), [D, TOWN])
        self.tap("spc", lambda: self.sp_c.ap(), [D, TOWN])
        self.tap("spv", lambda: self.sp_v.ap(), [TOWN, D])

    def _rwkv_pre_chunk(self, ch, w2sb1, w2sb2, maa1h, maa1l, d1h, d1l, d2h, d2l):
        P, AR, din, psum = self.P, self.AR, self.din, self.psum
        base = ch * CH
        AR.push()
        hch = AR.alloc(NB * (CH + 8), "hch")
        HW = CH + 8
        h3 = hch.rearrange("p (b c) -> p b c", b=NB)
        P.wait("sync", "dh")
        P.wait("sync", "dxh")
        for b in range(NB):
            P.dma("sync", h3[:, b, 0:CH],
                  self.sp_h[b * 128:(b + 1) * 128, base:base + CH], "dh")
            P.dma("sync", h3[:, b, CH:CH + 1],
                  self.sp_h[b * 128:(b + 1) * 128, 1024 + ch:1025 + ch], "dh",
                  allow_slow_non_contiguous=True)
        P.wait("vector", "dh")
        xx = AR.alloc(NB * CH, "xx")
        xx3 = xx.rearrange("p (b c) -> p b c", b=NB)
        xjh = AR.alloc_bf16(NB * CH, "xjh")
        xjl = AR.alloc_bf16(NB * CH, "xjl")
        xjh3 = xjh.rearrange("p (b c) -> p b c", b=NB)
        xjl3 = xjl.rearrange("p (b c) -> p b c", b=NB)
        aT1 = AR.alloc(CH, "aT1")
        aT2f = AR.alloc(CH, "aT2")
        xjt = AR.alloc(CH, "xjt")
        pjt = AR.alloc(CH, "pjt")
        wsA = AR.alloc(2 * 512, "wsA")
        wsH = AR.alloc_bf16(2 * 512, "wsH")
        wsL = AR.alloc_bf16(2 * 512, "wsL")
        self.ws_bufs = (wsA, wsH, wsL)
        if not hasattr(self, "ws_n"):
            self.ws_n = 0
            self.ws_marks = {0: [], 1: []}
            self.ev_n = 0
        ev = AR.alloc(2 * CH, "evac")
        zb = AR.alloc(3 * CH, "zbuf")
        zh = AR.alloc_bf16(CH, "zh")
        zl = AR.alloc_bf16(CH, "zl")

        P.op("vector", lambda E: E.tensor_tensor(
            out=xx3[:, :, 1:CH], in0=h3[:, :, 0:CH - 1],
            in1=h3[:, :, 1:CH], op=ALU.subtract))
        P.op("vector", lambda E: E.tensor_tensor(
            out=xx3[:, :, 0:1], in0=h3[:, :, CH:CH + 1],
            in1=h3[:, :, 0:1], op=ALU.subtract), inc="ve")
        P.drain("vector")

        for b in range(NB):
            self._mix_block(b, ch, h3, xx3, xjh3, xjl3, xjt, None,
                            self.vec("maa_x")[:, b:b + 1])
        P.wait("tensor", "ve")
        P.wait("tensor", "ae")
        for b in range(NB):
            lh = maa1h[:, b * 160:b * 160 + 128]
            ll = maa1l[:, b * 160:b * 160 + 128]
            lh2 = maa1h[:, b * 160 + 128:(b + 1) * 160]
            ll2 = maa1l[:, b * 160 + 128:(b + 1) * 160]
            rh = xjh3[:, b, :]
            rl = xjl3[:, b, :]
            st = (b == 0)
            sp = (b == NB - 1)
            P.op("tensor", lambda E, lh=lh, rh=rh, st=st: E.matmul(
                psum[0][:, :], lh, rh, start=st, stop=False))
            P.op("tensor", lambda E, lh=lh, rl=rl: E.matmul(
                psum[0][:, :], lh, rl, start=False, stop=False))
            P.op("tensor", lambda E, ll=ll, rh=rh, sp=sp: E.matmul(
                psum[0][:, :], ll, rh, start=False, stop=sp),
                inc=("pe" if sp else None))
            P.op("tensor", lambda E, lh2=lh2, rh=rh, st=st: E.matmul(
                psum[1][0:32, :], lh2, rh, start=st, stop=False))
            P.op("tensor", lambda E, lh2=lh2, rl=rl: E.matmul(
                psum[1][0:32, :], lh2, rl, start=False, stop=False))
            P.op("tensor", lambda E, ll2=ll2, rh=rh, sp=sp: E.matmul(
                psum[1][0:32, :], ll2, rh, start=False, stop=sp),
                inc=("pe" if sp else None))
        P.wait("scalar", "pe")
        P.op("scalar", lambda E: E.activation(out=aT1, in_=psum[0][:, :],
                                              func=AF.Tanh), inc="ae")
        P.op("scalar", lambda E: E.activation(out=aT2f[0:32, :], in_=psum[1][0:32, :],
                                              func=AF.Tanh), inc="ae")
        ae_a = P.mark("ae")

        for j, name in ((0, "w"), (1, "k"), (2, "v"), (3, "r"), (4, "g")):
            maa_j = self.vec("maa_" + name)
            P.wait("tensor", "ae", ae_a)
            P.wait("tensor", "vev")
            P.wait("tensor", "vdec")
            for b in range(NB):
                bank = 2 + (b % 2)
                if P.mark("vmj") >= 1:
                    P.wait("tensor", "vmj", P.mark("vmj") - 1)
                if j < 4:
                    lw = w2sb1[32 * j:32 * j + 32, b * 128:(b + 1) * 128]
                    ra = aT1[32 * j:32 * j + 32, :]
                else:
                    lw = w2sb2[:, b * 128:(b + 1) * 128]
                    ra = aT2f[0:32, :]
                tp = (96, 0) if j == 3 else None
                P.op("tensor", lambda E, lw=lw, ra=ra, bank=bank, tp=tp: E.matmul(
                    psum[bank][:, :], lw, ra, start=True, stop=True,
                    tile_position=tp), inc="pmj")
                P.wait("vector", "pmj")
                P.op("vector", lambda E, bank=bank, m=maa_j[:, b:b + 1]: E.tensor_scalar(
                    out=pjt, in0=psum[bank][:, :], scalar1=m, scalar2=None,
                    op0=ALU.add), inc="vmj")
                P.drain("vector")
                self._mix_block(b, ch, h3, xx3, xjh3, xjl3, xjt, pjt, None)
            if name == "w":
                self._dec_path(ch, xjh3, xjl3, d1h, d1l, d2h, d2l, zb, zh, zl, ev)
            elif name == "v":
                self._proj_tok(ch, xjh3, xjl3, din["wv"], self.sp_v, wsA, wsH, wsL, ev)
            else:
                dst = {"k": self.sp_k, "r": self.sp_r, "g": self.sp_g}[name]
                act = AF.Silu if name == "g" else None
                self._proj_fm(ch, xjh3, xjl3, din["w" + name], dst, wsA, wsH, wsL,
                              ev, act)
        P.wait("gpsimd", "dxea")
        P.wait("gpsimd", "dxeb")
        P.wait("gpsimd", "dxc")
        AR.pop()
        P.barrier()

    def _mix_block(self, b, ch, h3, xx3, xjh3, xjl3, xjt, pjt, maa_col):
        P = self.P
        hb = h3[:, b, 0:CH]
        if pjt is None:
            P.op("vector", lambda E, b=b: E.tensor_scalar(
                out=xjt, in0=xx3[:, b, :], scalar1=maa_col, scalar2=None,
                op0=ALU.mult))
        else:
            P.op("vector", lambda E, b=b: E.tensor_tensor(
                out=xjt, in0=xx3[:, b, :], in1=pjt, op=ALU.mult))
        P.drain("vector")
        P.op("vector", lambda E, hb=hb: E.tensor_tensor(
            out=xjt, in0=xjt, in1=hb, op=ALU.add), inc="ve")
        P.wait("scalar", "ve")
        P.op("scalar", lambda E, b=b: E.activation(
            out=xjh3[:, b, :], in_=xjt, func=AF.Identity), inc="ae")
        P.wait("vector", "ae")
        P.op("vector", lambda E, b=b: E.tensor_tensor(
            out=xjl3[:, b, :], in0=xjt, in1=xjh3[:, b, :], op=ALU.subtract), inc="ve")
        P.drain("vector")

    def _stream_w(self, wdram, rows, cols, wid):
        P = self.P
        ns = getattr(self, "ws_nslots", 2)
        stride = getattr(self, "ws_stride", 512)
        self.ws_n += 1
        slot = self.ws_n % ns
        dsem = "dws%d" % slot
        marks = self.ws_marks.setdefault(slot, [])
        if marks:
            P.wait("sync", "pwf", marks[-1])
        wsA, wsH, wsL = self.ws_bufs
        P.dma("sync", wsA[:, slot * stride:slot * stride + wid],
              wdram[rows, cols], dsem)
        P.wait("scalar", dsem)
        P.wait("vector", dsem)
        wh = wsH[:, slot * stride:slot * stride + wid]
        wl = wsL[:, slot * stride:slot * stride + wid]
        self.split_pair(wsA[:, slot * stride:slot * stride + wid], wh, wl,
                        ve="vws", ae="aws")
        P.wait("tensor", "vws")
        return wh, wl, slot

    def _evac_store(self, bank_ap, dst_rows, dst_cols, dst, ev, act=None):
        P = self.P
        n = self.ev_n
        self.ev_n += 1
        slot = n % 2
        dsem = "dxea" if slot == 0 else "dxeb"
        evslot = ev[:, slot * CH:slot * CH + CH]
        eng = "scalar" if act is not None else "vector"
        P.wait(eng, dsem)
        if act is not None:
            P.op("scalar", lambda E, o=evslot, i=bank_ap: E.activation(
                out=o, in_=i, func=act), inc="vev")
        else:
            P.op("vector", lambda E, o=evslot, i=bank_ap: E.tensor_copy(
                out=o, in_=i), inc="vev")
        P.wait("gpsimd", "vev")
        P.dma("gpsimd", dst[dst_rows, dst_cols], evslot, dsem)

    def _proj_fm(self, ch, xjh3, xjl3, wdram, dst, wsA, wsH, wsL, ev, act):
        P, psum = self.P, self.psum
        P.wait("tensor", "vmj")
        P.wait("tensor", "ve")
        P.wait("tensor", "ae")
        for q in range(4):
            for f in range(NB):
                wh, wl, slot = self._stream_w(
                    wdram, slice(f * 128, (f + 1) * 128),
                    slice(q * 512, q * 512 + 512), 512)
                rh = xjh3[:, f, :]
                rl = xjl3[:, f, :]
                st = (f == 0)
                sp = (f == NB - 1)
                for o in range(4):
                    whb = wh[:, o * 128:(o + 1) * 128]
                    wlb = wl[:, o * 128:(o + 1) * 128]
                    last = sp and (o == 3)
                    P.op("tensor", lambda E, w=whb, r=rh, o=o, st=st: E.matmul(
                        psum[o][:, :], w, r, start=st, stop=False))
                    P.op("tensor", lambda E, w=whb, r=rl, o=o: E.matmul(
                        psum[o][:, :], w, r, start=False, stop=False))
                    P.op("tensor", lambda E, w=wlb, r=rh, o=o, sp=sp: E.matmul(
                        psum[o][:, :], w, r, start=False, stop=sp), inc="pwf")
                self.ws_marks[slot].append(P.mark("pwf"))
            P.wait("vector", "pwf")
            P.wait("scalar", "pwf")
            for o in range(4):
                og = q * 4 + o
                self._evac_store(psum[o][:, :], slice(og * 128, (og + 1) * 128),
                                 slice(ch * CH, (ch + 1) * CH), dst, ev, act)
            P.wait("tensor", "vev")

    def _proj_tok(self, ch, xjh3, xjl3, wdram, dst, wsA, wsH, wsL, ev):
        P, psum = self.P, self.psum
        P.wait("tensor", "vmj")
        P.wait("tensor", "ve")
        P.wait("tensor", "ae")
        for osl in range(4):
            for f in range(NB):
                wh, wl, slot = self._stream_w(
                    wdram, slice(f * 128, (f + 1) * 128),
                    slice(osl * 512, (osl + 1) * 512), 512)
                st = (f == 0)
                sp = (f == NB - 1)
                for tb in range(4):
                    lh = xjh3[:, f, tb * 128:(tb + 1) * 128]
                    ll = xjl3[:, f, tb * 128:(tb + 1) * 128]
                    last = sp and (tb == 3)
                    P.op("tensor", lambda E, l=lh, w=wh, tb=tb, st=st: E.matmul(
                        psum[tb][:, :], l, w, start=st, stop=False))
                    P.op("tensor", lambda E, l=ll, w=wh, tb=tb: E.matmul(
                        psum[tb][:, :], l, w, start=False, stop=False))
                    P.op("tensor", lambda E, l=lh, w=wl, tb=tb, sp=sp: E.matmul(
                        psum[tb][:, :], l, w, start=False, stop=sp), inc="pwf")
                self.ws_marks[slot].append(P.mark("pwf"))
            P.wait("vector", "pwf")
            for tb in range(4):
                self._evac_store(psum[tb][:, :],
                                 slice(ch * CH + tb * 128, ch * CH + (tb + 1) * 128),
                                 slice(osl * 512, (osl + 1) * 512), dst, ev, None)
            P.wait("tensor", "vev")

    def _dec_path(self, ch, xjh3, xjl3, d1h, d1l, d2h, d2l, zb, zh, zl, ev):
        P, psum = self.P, self.psum
        P.wait("tensor", "vmj")
        P.wait("tensor", "ve")
        P.wait("tensor", "ae")
        zT = zb[0:64, 0:CH]
        wblk = zb[:, CH:2 * CH]
        eblk = zb[:, 2 * CH:3 * CH]
        for f in range(NB):
            lh = d1h[:, f * DDEC:(f + 1) * DDEC]
            ll = d1l[:, f * DDEC:(f + 1) * DDEC]
            rh = xjh3[:, f, :]
            rl = xjl3[:, f, :]
            st = (f == 0)
            sp = (f == NB - 1)
            P.op("tensor", lambda E, lh=lh, rh=rh, st=st: E.matmul(
                psum[0][0:64, :], lh, rh, start=st, stop=False))
            P.op("tensor", lambda E, lh=lh, rl=rl: E.matmul(
                psum[0][0:64, :], lh, rl, start=False, stop=False))
            P.op("tensor", lambda E, ll=ll, rh=rh, sp=sp: E.matmul(
                psum[0][0:64, :], ll, rh, start=False, stop=sp),
                inc=("pe" if sp else None))
        P.wait("scalar", "pe")
        P.op("scalar", lambda E: E.activation(out=zT, in_=psum[0][0:64, :],
                                              func=AF.Tanh), inc="ae")
        P.wait("vector", "ae")
        P.drain("scalar")
        self.split_pair(zT, zh[0:64, :], zl[0:64, :], ve="vz", ae="az")
        P.wait("tensor", "vz")
        td = self.vec("time_decay")
        for fo in range(NB):
            bank = 2 + (fo % 2)
            if fo >= 2:
                P.wait("tensor", "vdec", P.mark("vdec") - 1)
            lh = d2h[:, fo * 128:(fo + 1) * 128]
            ll = d2l[:, fo * 128:(fo + 1) * 128]
            P.op("tensor", lambda E, lh=lh, bank=bank: E.matmul(
                psum[bank][:, :], lh, zh[0:64, :], start=True, stop=False))
            P.op("tensor", lambda E, lh=lh, bank=bank: E.matmul(
                psum[bank][:, :], lh, zl[0:64, :], start=False, stop=False))
            P.op("tensor", lambda E, ll=ll, bank=bank: E.matmul(
                psum[bank][:, :], ll, zh[0:64, :], start=False, stop=True),
                inc="pdec")
            P.wait("vector", "pdec")
            P.op("vector", lambda E, bank=bank, m=td[:, fo:fo + 1]: E.tensor_scalar(
                out=wblk, in0=psum[bank][:, :], scalar1=m, scalar2=None, op0=ALU.add),
                inc="vdec")
            P.wait("scalar", "vdec")
            P.op("scalar", lambda E: E.activation(out=eblk, in_=wblk, func=AF.Exp),
                 inc="adec")
            P.wait("vector", "adec")
            for s in range(4):
                P.op("vector", lambda E, s=s: E.tensor_tensor_scan(
                    out=wblk[:, s * 128:(s + 1) * 128],
                    data0=eblk[:, s * 128:(s + 1) * 128],
                    data1=eblk[:, s * 128:(s + 1) * 128],
                    initial=0.0, op0=ALU.add, op1=ALU.bypass),
                    inc=("vc" if s == 3 else None))
            P.drain("vector")
            P.wait("gpsimd", "vc")
            P.dma("gpsimd", self.sp_c[fo * 128:(fo + 1) * 128, ch * CH:(ch + 1) * CH],
                  wblk, "dxc")
            P.wait("vector", "dxc")

    def fm_dram(self, t, cols):
        return t.ap().rearrange("(b p) t -> p b t", p=128)[:, :, cols]

    def stage_wkv(self):
        P, AR = self.P, self.AR
        AR.push()
        S_all = AR.alloc((RH // 2) * RN, "S_all")
        cacc = AR.alloc(NB, "cacc")
        cls = AR.alloc(8 * NB, "cls")
        S_low = AR.alloc(1024, "S_low")
        S_high = AR.alloc(1024, "S_high")
        S_work = AR.alloc(1024, "S_work")
        sg = AR.alloc(1024 + NB, "sgather")
        for ch in range(2):
            self._wkv_local_chunk(ch, S_all, cacc, cls)
        P.wait("gpsimd", "dxw")
        P.wait("gpsimd", "dsr")
        P.wait("gpsimd", "dsy")
        if self.wkv_parts >= 2:
            P.op("gpsimd", lambda E: E.collective_compute(
                "AllGather", ALU.bypass, replica_groups=[list(range(NCORE))],
                ins=[self.wkv_bounce.ap()], outs=[self.wkv_gat.ap()]), inc="cc")
            P.wait("gpsimd", "cc")
            P.barrier()
            self._wkv_combine(S_work, S_low, S_high, sg)
            P.barrier()
        if self.wkv_parts >= 3:
            for ch in range(2):
                self._wkv_final_chunk(ch, S_low if ch == 0 else S_high, cls)
            P.wait("gpsimd", "dxea")
            P.wait("gpsimd", "dxeb")
        AR.pop()
        P.barrier()
        self.tap("spy", lambda: self.sp_y.ap(), [D, TOWN])
        self.tap("spyc", lambda: self.sp_dbg.ap(), [D, TOWN])
        self.tap("spx1", lambda: self.sp_x1.ap(), [D, TOWN])

    def _wkv_local_chunk(self, ch, S_all, cacc, cls):
        P, AR, psum = self.P, self.AR, self.psum
        AR.push()
        rT = AR.alloc(D, "rT"); kT = AR.alloc(D, "kT")
        cT = AR.alloc(D, "cT"); vt = AR.alloc(D, "vt")
        eK = AR.alloc(D, "eK"); eR = AR.alloc(D, "eR")
        kt = AR.alloc(D, "kt"); rt = AR.alloc(D, "rt")
        khf = AR.alloc(D, "khf"); kht = AR.alloc(D, "kht")
        rku = AR.alloc(D, "rku")
        kth = AR.alloc_bf16(D, "kth"); ktl = AR.alloc_bf16(D, "ktl")
        rth = AR.alloc_bf16(D, "rth"); rtl = AR.alloc_bf16(D, "rtl")
        khh = AR.alloc_bf16(D, "khh"); khl = AR.alloc_bf16(D, "khl")
        vh = AR.alloc_bf16(D, "vh"); vl = AR.alloc_bf16(D, "vl")
        Sh = AR.alloc_bf16(1024, "Sh"); Sl = AR.alloc_bf16(1024, "Sl")
        Gs = AR.alloc(1024, "Gs")
        dtmp = AR.alloc(1024, "dtmp")
        Gh = AR.alloc_bf16(1024, "Gh"); Gl = AR.alloc_bf16(1024, "Gl")
        Dc = AR.alloc(32, "Dc")
        eL = AR.alloc(NB, "eL")
        ydump = AR.alloc(D, "ydump")
        c3 = cT.rearrange("p (b t) -> p b t", b=NB)
        e3 = eR.rearrange("p (b t) -> p b t", b=NB)
        kthp = [eK[:, 0:1024].bitcast(BF16), eK[:, 1024:2048].bitcast(BF16)]
        ktlp = [eR[:, 0:1024].bitcast(BF16), eR[:, 1024:2048].bitcast(BF16)]
        Shp = [cT[:, 0:512].bitcast(BF16), cT[:, 512:1024].bitcast(BF16)]
        Slp = [vt[:, 0:512].bitcast(BF16), vt[:, 512:1024].bitcast(BF16)]
        uvec = self.vec("u_faaaa")
        P.drain("vector")
        P.wait("vector", "dxw")
        P.op("vector", lambda E: E.memset(S_all[:], 0.0))
        P.op("vector", lambda E: E.memset(cacc[:], 0.0), inc="ve")
        for u in range(4):
            cbase = ch * CH + u * SUB
            csl = slice(cbase, cbase + SUB)
            P.wait("sync", "dsc")
            P.wait("sync", "pS")
            P.wait("sync", "ve")
            P.wait("sync", "ae")
            P.dma("sync", rT.rearrange("p (b t) -> p b t", b=NB),
                  self.fm_dram(self.sp_r, csl), "dsc")
            P.dma("sync", kT.rearrange("p (b t) -> p b t", b=NB),
                  self.fm_dram(self.sp_k, csl), "dsc")
            P.dma("sync", cT.rearrange("p (b t) -> p b t", b=NB),
                  self.fm_dram(self.sp_c, csl), "dsc")
            P.dma("sync", vt[:], self.sp_v.ap()[csl, :], "dsc")
            P.wait("scalar", "dsc")
            P.wait("vector", "dsc")
            if self.wkv_fine & 2:
                P.wait("scalar", "vSu")
                P.wait("scalar", "vGm")
                P.op("scalar", lambda E: E.activation(out=eK, in_=cT, func=AF.Exp),
                     inc="ae")
                P.op("scalar", lambda E: E.activation(
                    out=e3[:, :, 1:SUB], in_=c3[:, :, 0:SUB - 1], func=AF.Exp,
                    scale=-1.0), inc="ae")
                P.op("scalar", lambda E: E.activation(
                    out=eL, in_=c3[:, :, SUB - 1:SUB].rearrange("p b t -> p (b t)"),
                    func=AF.Exp, scale=-1.0), inc="ae")
                P.op("vector", lambda E: E.memset(e3[:, :, 0:1], 1.0), inc="ve")
            if self.wkv_fine & 4:
                cLs = c3[:, :, SUB - 1:SUB].rearrange("p b t -> p (b t)")
                P.drain("vector")
                P.op("vector", lambda E: E.tensor_tensor(out=cacc[:], in0=cacc[:],
                                                         in1=cLs, op=ALU.add))
                P.op("vector", lambda E, ch=ch, u=u: E.tensor_copy(
                    out=cls[:, (ch * 4 + u) * NB:(ch * 4 + u + 1) * NB], in_=cLs),
                    inc="ve")
                P.op("vector", lambda E: E.tensor_tensor(out=rku, in0=rT, in1=kT,
                                                         op=ALU.mult))
                P.drain("vector")
                ub = uvec.unsqueeze(2).broadcast_to([128, NB, SUB])
                P.op("vector", lambda E: E.tensor_tensor(
                    out=rku.rearrange("p (b t) -> p b t", b=NB),
                    in0=rku.rearrange("p (b t) -> p b t", b=NB), in1=ub,
                    op=ALU.mult), inc="ve")
                P.wait("vector", "ae")
                P.op("vector", lambda E: E.tensor_tensor(out=kt, in0=kT, in1=eK,
                                                         op=ALU.mult))
                P.op("vector", lambda E: E.tensor_tensor(out=rt, in0=rT, in1=eR,
                                                         op=ALU.mult), inc="ve")
                P.drain("vector")
                eLb = eL.unsqueeze(2).broadcast_to([128, NB, SUB])
                P.op("vector", lambda E: E.tensor_tensor(
                    out=khf.rearrange("p (b t) -> p b t", b=NB),
                    in0=kt.rearrange("p (b t) -> p b t", b=NB), in1=eLb,
                    op=ALU.mult), inc="ve")
                P.drain("vector")
            if self.wkv_fine & 8:
                P.wait("vector", "dsr")
                P.wait("scalar", "dsr")
                P.wait("vector", "pS")
                P.wait("scalar", "pS")
                P.wait("scalar", "ve")
                self.split_pair(kt, kth, ktl)
                self.split_pair(rt, rth, rtl)
                self.split_pair(vt, vh, vl)
                P.drain("vector")
                for par in range(2):
                    msk = self.sb_ind2[:, par:par + 1]
                    P.op("vector", lambda E, o=kthp[par], msk=msk: E.tensor_scalar(
                        out=o[:, :], in0=kth, scalar1=msk, scalar2=None,
                        op0=ALU.mult))
                    P.op("vector", lambda E, o=ktlp[par], msk=msk: E.tensor_scalar(
                        out=o[:, :], in0=ktl, scalar1=msk, scalar2=None,
                        op0=ALU.mult), inc=("ve" if par == 1 else None))
                P.drain("vector")
            if self.wkv_fine & 16:
                P.wait("gpsimd", "ve")
                P.wait("gpsimd", "ae")
                P.dma("gpsimd", self.fm_dram(self.sp_rth, csl),
                      rth.rearrange("p (b t) -> p b t", b=NB), "dsr")
                P.dma("gpsimd", self.fm_dram(self.sp_rtl, csl),
                      rtl.rearrange("p (b t) -> p b t", b=NB), "dsr")
            if self.wkv_sub >= 2:
                P.wait("tensor", "ve")
                P.wait("tensor", "vev")
                for b in range(NB):
                    P.op("tensor", lambda E, b=b: E.transpose(
                        psum[b // 4][:, (b % 4) * 128:(b % 4 + 1) * 128],
                        khf[:, b * 128:(b + 1) * 128], self.sb_ident[:]),
                        inc=("ptp" if b == NB - 1 else None))
                P.wait("vector", "ptp")
                for q in range(4):
                    P.op("vector", lambda E, q=q: E.tensor_copy(
                        out=kht[:, q * 512:(q + 1) * 512], in_=psum[q][:, :]),
                        inc="vev")
                P.drain("vector")
                P.wait("scalar", "vev")
                self.split_pair(kht, khh, khl)
            if self.wkv_sub >= 3:
                P.wait("tensor", "vev")
                P.wait("tensor", "vy")
                for b in range(NB):
                    P.op("tensor", lambda E, b=b: E.matmul(
                        psum[4][:, 2 * b:2 * b + 2], rku[:, b * 128:(b + 1) * 128],
                        self.sb_ind2[:], start=True, stop=True),
                        inc=("pd" if b == NB - 1 else None))
                P.wait("vector", "pd")
                P.op("vector", lambda E: E.tensor_copy(out=Dc[:, 0:32],
                                                       in_=psum[4][:, 0:32]), inc="vD")
            if u > 0:
                P.drain("vector")
                P.wait("scalar", "vSu")
                self.split_pair(S_all, Sh, Sl, ve="vS", ae="aS")
                P.drain("vector")
                for par in range(2):
                    msk = self.sb_ind2[:, par:par + 1]
                    P.op("vector", lambda E, o=Shp[par], msk=msk: E.tensor_scalar(
                        out=o[:, :], in0=Sh[:, :], scalar1=msk, scalar2=None,
                        op0=ALU.mult))
                    P.op("vector", lambda E, o=Slp[par], msk=msk: E.tensor_scalar(
                        out=o[:, :], in0=Sl[:, :], scalar1=msk, scalar2=None,
                        op0=ALU.mult), inc=("vS" if par == 1 else None))
                P.drain("vector")
            if self.wkv_sub >= 4:
                P.wait("tensor", "vD")
                P.wait("tensor", "aG")
                for wave in range(4):
                    gb = (0, 2, 0, 2)[wave]
                    if wave >= 2:
                        P.wait("tensor", "vG", P.mark("vG") - 1)
                    for hh in range(8):
                        hd = wave * 8 + hh
                        blk = hd // 2
                        par = hd % 2
                        lk_h = kthp[par][:, blk * 128:(blk + 1) * 128]
                        lk_l = ktlp[par][:, blk * 128:(blk + 1) * 128]
                        rr_h = rth[:, blk * 128:(blk + 1) * 128]
                        rr_l = rtl[:, blk * 128:(blk + 1) * 128]
                        pg = psum[gb + hh // 4][:, (hh % 4) * 128:(hh % 4 + 1) * 128]
                        P.op("tensor", lambda E, a=lk_h, c=rr_h, pg=pg: E.matmul(
                            pg, a, c, start=True, stop=False))
                        P.op("tensor", lambda E, a=lk_h, c=rr_l, pg=pg: E.matmul(
                            pg, a, c, start=False, stop=False))
                        P.op("tensor", lambda E, a=lk_l, c=rr_h, pg=pg: E.matmul(
                            pg, a, c, start=False, stop=True),
                            inc=("pG" if hh == 7 else None))
                    if self.wkv_fine & 32:
                        P.wait("vector", "pG")
                        P.wait("vector", "aG")
                        mky = self.sb_wkvmask[:, :].unsqueeze(1).broadcast_to([128, 8, 128])
                        idb = self.sb_ident[:, :].unsqueeze(1).broadcast_to([128, 8, 128])
                        dcb = Dc[:, wave * 8:(wave + 1) * 8].unsqueeze(2).broadcast_to(
                            [128, 8, 128])
                        g3 = Gs.rearrange("p (h t) -> p h t", h=8)
                        d3 = dtmp.rearrange("p (h t) -> p h t", h=8)
                        for half in range(2):
                            P.op("vector", lambda E, half=half, gb=gb: E.tensor_tensor(
                                out=g3[:, half * 4:(half + 1) * 4, :],
                                in0=psum[gb + half].ap().rearrange("p (h t) -> p h t", h=4),
                                in1=mky[:, half * 4:(half + 1) * 4, :], op=ALU.mult))
                        P.op("vector", lambda E, dcb=dcb: E.tensor_tensor(
                            out=d3, in0=idb, in1=dcb, op=ALU.mult))
                        P.drain("vector")
                        P.op("vector", lambda E: E.tensor_tensor(
                            out=g3, in0=g3, in1=d3, op=ALU.add), inc="vGm")
                        P.drain("vector")
                        P.wait("scalar", "vGm")
                        P.wait("scalar", "pyw")
                        P.wait("vector", "pyw")
                        self.split_pair(Gs, Gh, Gl, ve="vG", ae="aG")
                    if self.wkv_fine & 64:
                        P.wait("tensor", "vG")
                        if u > 0:
                            P.wait("tensor", "vS")
                        for hh in range(8):
                            hd = wave * 8 + hh
                            blk = hd // 2
                            ro = 64 * (hd % 2)
                            yp = psum[4 + blk // 4][ro:ro + 64,
                                                    (blk % 4) * 128:(blk % 4 + 1) * 128]
                            gh_s = Gh[:, hh * 128:(hh + 1) * 128]
                            gl_s = Gl[:, hh * 128:(hh + 1) * 128]
                            v_h = vh[:, hd * 64:(hd + 1) * 64]
                            v_l = vl[:, hd * 64:(hd + 1) * 64]
                            tpv = (0, ro)
                            last_wave = (hh == 7)
                            P.op("tensor", lambda E, a=v_h, c=gh_s, yp=yp, tp=tpv: E.matmul(
                                yp, a, c, start=True, stop=False, tile_position=tp))
                            P.op("tensor", lambda E, a=v_h, c=gl_s, yp=yp, tp=tpv: E.matmul(
                                yp, a, c, start=False, stop=False, tile_position=tp))
                            P.op("tensor", lambda E, a=v_l, c=gh_s, yp=yp, tp=tpv, lu=(u == 0):
                                 E.matmul(yp, a, c, start=False, stop=lu, tile_position=tp),
                                 inc=("pyw" if (last_wave and u == 0) else None))
                            if u > 0:
                                par = hd % 2
                                s_h = Shp[par][:, blk * 64:(blk + 1) * 64]
                                s_l = Slp[par][:, blk * 64:(blk + 1) * 64]
                                r_h = rth[:, blk * 128:(blk + 1) * 128]
                                r_l = rtl[:, blk * 128:(blk + 1) * 128]
                                tps = (0, ro)
                                P.op("tensor", lambda E, a=s_h, c=r_h, yp=yp, tp=tps:
                                     E.matmul(yp, a, c, start=False, stop=False,
                                              tile_position=tp))
                                P.op("tensor", lambda E, a=s_h, c=r_l, yp=yp, tp=tps:
                                     E.matmul(yp, a, c, start=False, stop=False,
                                              tile_position=tp))
                                P.op("tensor", lambda E, a=s_l, c=r_h, yp=yp, tp=tps:
                                     E.matmul(yp, a, c, start=False, stop=True,
                                              tile_position=tp),
                                     inc=("pyw" if last_wave else None))
                if self.wkv_sub >= 5:
                    P.wait("vector", "pyw")
                    P.wait("vector", "dsy")
                    for q in range(4):
                        P.op("vector", lambda E, q=q: E.tensor_copy(
                            out=ydump[:, q * 512:(q + 1) * 512], in_=psum[4 + q][:, :]),
                            inc=("vy" if q == 3 else None))
                    P.wait("gpsimd", "vy")
                    P.dma("gpsimd", self.fm_dram(self.sp_y, csl),
                          ydump.rearrange("p (b t) -> p b t", b=NB), "dsy")
            if self.wkv_sub >= 6:
                P.wait("tensor", "vev")
                P.wait("tensor", "aS" if u > 0 else "vev")
                for hd in range(RH):
                    blk = hd // 2
                    ro = 64 * (hd % 2)
                    sp_ = psum[blk // 8][ro:ro + 64, (blk % 8) * 64:(blk % 8 + 1) * 64]
                    k_h = khh[:, hd * 64:(hd + 1) * 64]
                    k_l = khl[:, hd * 64:(hd + 1) * 64]
                    v_h = vh[:, hd * 64:(hd + 1) * 64]
                    v_l = vl[:, hd * 64:(hd + 1) * 64]
                    tpu = (0, ro)
                    P.op("tensor", lambda E, a=k_h, c=v_h, sp_=sp_, tp=tpu: E.matmul(
                        sp_, a, c, start=True, stop=False, tile_position=tp))
                    P.op("tensor", lambda E, a=k_h, c=v_l, sp_=sp_, tp=tpu: E.matmul(
                        sp_, a, c, start=False, stop=False, tile_position=tp))
                    P.op("tensor", lambda E, a=k_l, c=v_h, sp_=sp_, tp=tpu: E.matmul(
                        sp_, a, c, start=False, stop=True, tile_position=tp),
                        inc=("pS" if hd == RH - 1 else None))
                P.wait("vector", "pS")
                if u > 0:
                    P.wait("vector", "aS")
                eLb2 = eL.unsqueeze(2).broadcast_to([128, NB, RN])
                s3 = S_all.rearrange("p (b i) -> p b i", b=NB)
                P.drain("vector")
                P.op("vector", lambda E: E.tensor_tensor(out=s3, in0=s3, in1=eLb2,
                                                         op=ALU.mult))
                P.drain("vector")
                for q in range(2):
                    P.op("vector", lambda E, q=q: E.tensor_tensor(
                        out=S_all[:, q * 512:(q + 1) * 512],
                        in0=S_all[:, q * 512:(q + 1) * 512],
                        in1=psum[q][:, :], op=ALU.add),
                        inc=("vSu" if q == 1 else None))
                P.drain("vector")
                P.wait("tensor", "vSu")
        P.wait("gpsimd", "vSu")
        P.dma("gpsimd", self.wkv_bounce.ap()[ch, :, 0:1024], S_all[:], "dxw")
        P.dma("gpsimd", self.wkv_bounce.ap()[ch, :, 1024:1040], cacc[:], "dxw")
        AR.pop()

    def _wkv_combine(self, S_work, S_low, S_high, sg):
        P, AR = self.P, self.AR
        sel = self.sb_selw
        P.drain("vector")
        P.op("vector", lambda E: E.memset(S_work[:], 0.0))
        P.op("vector", lambda E: E.memset(S_low[:], 0.0))
        P.op("vector", lambda E: E.memset(S_high[:], 0.0), inc="ve")
        P.drain("vector")
        AR.push()
        t1 = AR.alloc(1024, "cmb_t1")
        ea = AR.alloc(NB, "cmb_ea")
        for c in range(15):
            row = 2 * _owner(c) + (0 if c <= 7 else 1)
            P.wait("sync", "dgw")
            P.wait("sync", "ve")
            P.wait("sync", "ae")
            P.dma("sync", sg[:, 0:1040], self.wkv_gat.ap()[row, :, :], "dgw")
            P.wait("scalar", "dgw")
            P.op("scalar", lambda E: E.activation(out=ea, in_=sg[:, 1024:1040],
                                                  func=AF.Exp, scale=-1.0), inc="ae")
            P.wait("vector", "ae")
            P.wait("vector", "dgw")
            eb = ea.unsqueeze(2).broadcast_to([128, NB, RN])
            sw3 = S_work.rearrange("p (b i) -> p b i", b=NB)
            P.op("vector", lambda E: E.tensor_tensor(out=sw3, in0=sw3, in1=eb,
                                                     op=ALU.mult))
            P.drain("vector")
            P.op("vector", lambda E: E.tensor_tensor(out=S_work[:], in0=S_work[:],
                                                     in1=sg[:, 0:1024], op=ALU.add))
            P.drain("vector")
            P.op("vector", lambda E, c=c: E.tensor_scalar(
                out=t1, in0=S_work[:], scalar1=sel[:, c:c + 1], scalar2=None,
                op0=ALU.mult))
            P.op("vector", lambda E, c=c: E.tensor_scalar(
                out=S_low[:], in0=S_low[:], scalar1=sel[:, 16 + c:17 + c],
                scalar2=None, op0=ALU.mult))
            P.drain("vector")
            P.op("vector", lambda E: E.tensor_tensor(out=S_low[:], in0=S_low[:],
                                                     in1=t1, op=ALU.add))
            P.drain("vector")
            P.op("vector", lambda E, c=c: E.tensor_scalar(
                out=t1, in0=S_work[:], scalar1=sel[:, 32 + c:33 + c], scalar2=None,
                op0=ALU.mult))
            P.op("vector", lambda E, c=c: E.tensor_scalar(
                out=S_high[:], in0=S_high[:], scalar1=sel[:, 48 + c:49 + c],
                scalar2=None, op0=ALU.mult))
            P.drain("vector")
            P.op("vector", lambda E: E.tensor_tensor(out=S_high[:], in0=S_high[:],
                                                     in1=t1, op=ALU.add), inc="ve")
            P.drain("vector")
        AR.pop()

    def _wkv_final_chunk(self, ch, S_in, cls):
        P, AR, psum, din = self.P, self.AR, self.psum, self.din
        AR.push()
        pref = AR.alloc(NB, "pref")
        escl = AR.alloc(NB, "escl")
        Ssc = AR.alloc(1024, "Ssc")
        Sh = AR.alloc_bf16(1024, "Sch"); Sl = AR.alloc_bf16(1024, "Scl")
        rthb = AR.alloc_bf16(D, "rthb"); rtlb = AR.alloc_bf16(D, "rtlb")
        yf = AR.alloc(D, "yf")
        ysq = AR.alloc(D, "ysq")
        gld = AR.alloc(D, "gld")
        stt = AR.alloc(2 * D, "stt")
        yngh = AR.alloc_bf16(NB * CH, "yngh")
        yngl = AR.alloc_bf16(NB * CH, "yngl")
        xrb = AR.alloc(2 * CH, "xrb")
        wsA = AR.alloc(2 * 512, "wsA2")
        wsH = AR.alloc_bf16(2 * 512, "wsH2")
        wsL = AR.alloc_bf16(2 * 512, "wsL2")
        ev = AR.alloc(2 * CH, "ev2")
        self.ws_bufs = (wsA, wsH, wsL)
        P.wait("scalar", "pwf")
        P.wait("vector", "pwf")
        P.wait("sync", "pwf")
        gam = self.vec("ln_gamma")
        bet = self.vec("ln_beta")
        yngh3 = yngh.rearrange("p (b t) -> p b t", b=NB)
        yngl3 = yngl.rearrange("p (b t) -> p b t", b=NB)
        Shp = [stt[:, 0:512].bitcast(BF16), stt[:, 512:1024].bitcast(BF16)]
        Slp = [stt[:, 1024:1536].bitcast(BF16), stt[:, 1536:2048].bitcast(BF16)]
        for u in range(4):
            cbase = ch * CH + u * SUB
            csl = slice(cbase, cbase + SUB)
            tsl = slice(u * SUB, (u + 1) * SUB)
            P.drain("vector")
            if u == 0:
                P.op("vector", lambda E: E.memset(pref[:], 0.0), inc="ve")
            else:
                P.op("vector", lambda E, u=u, ch=ch: E.tensor_tensor(
                    out=pref[:], in0=pref[:],
                    in1=cls[:, (ch * 4 + u - 1) * NB:(ch * 4 + u) * NB],
                    op=ALU.add), inc="ve")
            P.wait("scalar", "ve")
            P.op("scalar", lambda E: E.activation(out=escl, in_=pref, func=AF.Exp,
                                                  scale=-1.0), inc="ae")
            P.wait("vector", "ae")
            eb = escl.unsqueeze(2).broadcast_to([128, NB, RN])
            P.op("vector", lambda E: E.tensor_tensor(
                out=Ssc.rearrange("p (b i) -> p b i", b=NB),
                in0=S_in.rearrange("p (b i) -> p b i", b=NB), in1=eb,
                op=ALU.mult), inc="ve")
            P.drain("vector")
            P.wait("scalar", "ve")
            self.split_pair(Ssc, Sh, Sl, ve="vS2", ae="aS2")
            P.drain("vector")
            for par in range(2):
                msk = self.sb_ind2[:, par:par + 1]
                P.op("vector", lambda E, par=par, msk=msk: E.tensor_scalar(
                    out=Shp[par][:, :], in0=Sh[:, :], scalar1=msk, scalar2=None,
                    op0=ALU.mult))
                P.op("vector", lambda E, par=par, msk=msk: E.tensor_scalar(
                    out=Slp[par][:, :], in0=Sl[:, :], scalar1=msk, scalar2=None,
                    op0=ALU.mult), inc=("vS2" if par == 1 else None))
            P.drain("vector")
            P.wait("sync", "dcl")
            P.wait("sync", "pgb")
            P.wait("sync", "ve")
            P.wait("sync", "ae")
            P.dma("sync", rthb.rearrange("p (b t) -> p b t", b=NB),
                  self.fm_dram(self.sp_rth, csl), "dcl")
            P.dma("sync", rtlb.rearrange("p (b t) -> p b t", b=NB),
                  self.fm_dram(self.sp_rtl, csl), "dcl")
            P.dma("sync", yf.rearrange("p (b t) -> p b t", b=NB),
                  self.fm_dram(self.sp_y, csl), "dcl")
            P.dma("sync", gld.rearrange("p (b t) -> p b t", b=NB),
                  self.fm_dram(self.sp_g, csl), "dcl")
            P.wait("tensor", "vS2")
            P.wait("tensor", "dcl")
            P.wait("tensor", "vyf")
            P.wait("tensor", "vev")
            for hd in range(RH):
                blk = hd // 2
                ro = 64 * (hd % 2)
                par = hd % 2
                yp = psum[blk // 4][ro:ro + 64, (blk % 4) * 128:(blk % 4 + 1) * 128]
                tp = (0, ro)
                s_h = Shp[par][:, blk * 64:(blk + 1) * 64]
                s_l = Slp[par][:, blk * 64:(blk + 1) * 64]
                r_h = rthb[:, blk * 128:(blk + 1) * 128]
                r_l = rtlb[:, blk * 128:(blk + 1) * 128]
                P.op("tensor", lambda E, a=s_h, c=r_h, yp=yp, tp=tp: E.matmul(
                    yp, a, c, start=True, stop=False, tile_position=tp))
                P.op("tensor", lambda E, a=s_h, c=r_l, yp=yp, tp=tp: E.matmul(
                    yp, a, c, start=False, stop=False, tile_position=tp))
                P.op("tensor", lambda E, a=s_l, c=r_h, yp=yp, tp=tp: E.matmul(
                    yp, a, c, start=False, stop=True, tile_position=tp),
                    inc=("pyc" if hd == RH - 1 else None))
            P.wait("vector", "pyc")
            for q in range(4):
                P.op("vector", lambda E, q=q: E.tensor_tensor(
                    out=yf[:, q * 512:(q + 1) * 512],
                    in0=yf[:, q * 512:(q + 1) * 512], in1=psum[q][:, :],
                    op=ALU.add), inc=("vyf" if q == 3 else None))
            P.drain("vector")
            if "spyc" in self.dbg:
                P.wait("gpsimd", "vyf")
                P.dma("gpsimd", self.fm_dram(self.sp_dbg, csl),
                      yf.rearrange("p (b t) -> p b t", b=NB), "dbgy")
                P.wait("vector", "dbgy")
            P.op("vector", lambda E: E.tensor_tensor(out=ysq, in0=yf, in1=yf,
                                                     op=ALU.mult), inc="vyq")
            P.wait("tensor", "vyf")
            P.wait("tensor", "vyq")
            for b in range(NB):
                P.op("tensor", lambda E, b=b: E.matmul(
                    psum[b // 4][0:2, (b % 4) * 128:(b % 4 + 1) * 128],
                    self.sb_ind2[:], yf[:, b * 128:(b + 1) * 128],
                    start=True, stop=True))
                P.op("tensor", lambda E, b=b: E.matmul(
                    psum[4 + b // 4][0:2, (b % 4) * 128:(b % 4 + 1) * 128],
                    self.sb_ind2[:], ysq[:, b * 128:(b + 1) * 128],
                    start=True, stop=True),
                    inc=("pgn" if b == NB - 1 else None))
            P.wait("vector", "pgn")
            for q in range(4):
                P.op("vector", lambda E, q=q: E.tensor_copy(
                    out=stt[0:2, q * 512:(q + 1) * 512], in_=psum[q][0:2, :]))
                P.op("vector", lambda E, q=q: E.tensor_copy(
                    out=stt[0:2, D + q * 512:D + (q + 1) * 512],
                    in_=psum[4 + q][0:2, :]), inc=("vst" if q == 3 else None))
            P.drain("vector")
            P.op("vector", lambda E: E.tensor_scalar(
                out=stt[0:2, 0:D], in0=stt[0:2, 0:D], scalar1=1.0 / RN,
                scalar2=None, op0=ALU.mult))
            P.drain("vector")
            P.op("vector", lambda E: E.tensor_tensor(
                out=ysq[0:2, 0:D], in0=stt[0:2, 0:D], in1=stt[0:2, 0:D],
                op=ALU.mult))
            P.drain("vector")
            P.op("vector", lambda E: E.scalar_tensor_tensor(
                out=stt[0:2, D:2 * D], in0=stt[0:2, D:2 * D], scalar=1.0 / RN,
                in1=ysq[0:2, 0:D], op0=ALU.mult, op1=ALU.subtract), inc="vst")
            P.wait("scalar", "vst")
            P.op("scalar", lambda E: E.activation(
                out=stt[0:2, D:2 * D], in_=stt[0:2, D:2 * D], func=AF.Sqrt,
                bias=self.sb_cst[0:2, 1:2]), inc="ae")
            P.wait("vector", "ae")
            P.op("vector", lambda E: E.reciprocal(
                out=stt[0:2, D:2 * D], in_=stt[0:2, D:2 * D]), inc="vst")
            P.wait("tensor", "vst")
            for b in range(NB):
                P.op("tensor", lambda E, b=b: E.matmul(
                    psum[b // 4][:, (b % 4) * 128:(b % 4 + 1) * 128],
                    self.sb_ind2t[0:2, :], stt[0:2, b * 128:(b + 1) * 128],
                    start=True, stop=True))
                P.op("tensor", lambda E, b=b: E.matmul(
                    psum[4 + b // 4][:, (b % 4) * 128:(b % 4 + 1) * 128],
                    self.sb_ind2t[0:2, :], stt[0:2, D + b * 128:D + (b + 1) * 128],
                    start=True, stop=True),
                    inc=("pgb" if b == NB - 1 else None))
            P.wait("vector", "pgb")
            for q in range(4):
                P.op("vector", lambda E, q=q: E.tensor_tensor(
                    out=yf[:, q * 512:(q + 1) * 512],
                    in0=yf[:, q * 512:(q + 1) * 512], in1=psum[q][:, :],
                    op=ALU.subtract))
            P.drain("vector")
            for q in range(4):
                P.op("vector", lambda E, q=q: E.tensor_tensor(
                    out=yf[:, q * 512:(q + 1) * 512],
                    in0=yf[:, q * 512:(q + 1) * 512], in1=psum[4 + q][:, :],
                    op=ALU.mult))
            P.drain("vector")
            gmb = gam.unsqueeze(2).broadcast_to([128, NB, SUB])
            btb = bet.unsqueeze(2).broadcast_to([128, NB, SUB])
            y3 = yf.rearrange("p (b t) -> p b t", b=NB)
            P.op("vector", lambda E: E.tensor_tensor(out=y3, in0=y3, in1=gmb,
                                                     op=ALU.mult))
            P.drain("vector")
            P.op("vector", lambda E: E.tensor_tensor(out=y3, in0=y3, in1=btb,
                                                     op=ALU.add))
            P.drain("vector")
            P.op("vector", lambda E: E.tensor_tensor(out=yf, in0=yf, in1=gld,
                                                     op=ALU.mult), inc="ve")
            P.drain("vector")
            P.wait("scalar", "ve")
            P.op("scalar", lambda E, o=yngh3[:, :, tsl]: E.activation(
                out=o, in_=y3, func=AF.Identity), inc="ae")
            P.wait("vector", "ae")
            P.op("vector", lambda E, o=yngl3[:, :, tsl], h=yngh3[:, :, tsl]:
                 E.tensor_tensor(out=o, in0=y3, in1=h, op=ALU.subtract), inc="ve")
            P.drain("vector")
        P.wait("tensor", "ve")
        P.wait("tensor", "ae")
        P.wait("tensor", "pgb")
        ga = self.modv(2)
        nxr = [0]
        for q in range(4):
            for f in range(NB):
                wh, wl, slot = self._stream_w(
                    din["wo"], slice(f * 128, (f + 1) * 128),
                    slice(q * 512, q * 512 + 512), 512)
                rh = yngh3[:, f, :]
                rl = yngl3[:, f, :]
                st = (f == 0)
                sp = (f == NB - 1)
                for o in range(4):
                    whb = wh[:, o * 128:(o + 1) * 128]
                    wlb = wl[:, o * 128:(o + 1) * 128]
                    P.op("tensor", lambda E, w=whb, r=rh, o=o, st=st: E.matmul(
                        psum[o][:, :], w, r, start=st, stop=False))
                    P.op("tensor", lambda E, w=whb, r=rl, o=o: E.matmul(
                        psum[o][:, :], w, r, start=False, stop=False))
                    P.op("tensor", lambda E, w=wlb, r=rh, o=o, sp=sp: E.matmul(
                        psum[o][:, :], w, r, start=False, stop=sp), inc="pwf")
                self.ws_marks[slot].append(P.mark("pwf"))
            P.wait("vector", "pwf")
            for o in range(4):
                og = q * 4 + o
                P.wait("sync", "dxr")
                if P.mark("vev") >= 1:
                    P.wait("sync", "vev", P.mark("vev") - 1)
                P.dma("sync", xrb[:, (o % 2) * CH:(o % 2) * CH + CH],
                      din["xslab"][og * 128:(og + 1) * 128, ch * CH:(ch + 1) * CH],
                      "dxr")
                P.wait("vector", "dxr")
                n = self.ev_n
                self.ev_n += 1
                slot2 = n % 2
                dsem = "dxea" if slot2 == 0 else "dxeb"
                evslot = ev[:, slot2 * CH:slot2 * CH + CH]
                P.wait("vector", dsem)
                P.op("vector", lambda E, o=o, og=og, evs=evslot: E.scalar_tensor_tensor(
                    out=evs, in0=psum[o][:, :], scalar=ga[:, og:og + 1],
                    in1=xrb[:, (o % 2) * CH:(o % 2) * CH + CH],
                    op0=ALU.mult, op1=ALU.add), inc="vev")
                P.wait("gpsimd", "vev")
                P.dma("gpsimd", self.sp_x1[og * 128:(og + 1) * 128,
                                           ch * CH:(ch + 1) * CH], evslot, dsem)
            P.wait("tensor", "vev")
        AR.pop()

    def stage_ffn_pre(self):
        P, AR, din, psum = self.P, self.AR, self.din, self.psum
        AR.push()
        xb = AR.alloc(NB * TOWN, "xb2")
        AR.push()
        h2 = AR.alloc(NB * TOWN, "h2b")
        sq2 = AR.alloc(2 * 512, "sq2")
        st2 = AR.alloc(6 * 512, "st2")
        hm_h = AR.alloc_bf16(NB * 512, "hmh")
        hm_l = AR.alloc_bf16(NB * 512, "hml")
        alsb = AR.alloc(NB * AH, "alsb")
        lsp = AR.alloc(512, "lorspill")
        mu = st2[0:1, 0:1024]
        rstd = st2[0:1, 1024:2048]
        sc1 = st2[0:1, 2048:3072]
        P.wait("sync", "dxea")
        P.wait("sync", "dxeb")
        for b in range(NB):
            P.dma("sync", xb[:, b * TOWN:(b + 1) * TOWN],
                  self.sp_x1[b * 128:(b + 1) * 128, :], "df1")
            P.dma("sync", alsb[:, b * AH:(b + 1) * AH],
                  din["a_lora"][b * 128:(b + 1) * 128, :], "df1")
        P.wait("tensor", "df1")
        P.wait("scalar", "df1")
        P.wait("vector", "df1")
        cb1 = self.modv(4)
        sbv = self.modv(3)
        ln1 = self.vec("ffn_ln1")
        ln2 = self.vec("ffn_ln2")
        nsq = P.mark("pesq")
        for (cs, ce) in ((0, 512), (512, 1024)):
            W = ce - cs
            P.wait("tensor", "ve")
            for b in range(NB):
                xblk = xb[:, b * TOWN + cs:b * TOWN + ce]
                nsq += 1
                if nsq > 2:
                    P.wait("scalar", "pesq", nsq - 2)
                P.op("scalar", lambda E, x=xblk, b=b, W=W: E.activation(
                    out=sq2[:, (b % 2) * 512:(b % 2) * 512 + W], in_=x,
                    func=AF.Square), inc="ae")
                P.op("tensor", lambda E, x=xblk, b=b, W=W: E.matmul(
                    psum[0][0:1, 0:W], self.sb_ones[:, 0:1], x,
                    start=(b == 0), stop=(b == NB - 1)))
                P.wait("tensor", "ae")
                P.op("tensor", lambda E, b=b, W=W: E.matmul(
                    psum[1][0:1, 0:W], self.sb_ones[:, 0:1],
                    sq2[:, (b % 2) * 512:(b % 2) * 512 + W],
                    start=(b == 0), stop=(b == NB - 1)), inc="pesq")
            P.wait("vector", "pesq")
            P.op("vector", lambda E, cs=cs, ce=ce, W=W: E.tensor_scalar_mul(
                out=mu[:, cs:ce], in0=psum[0][0:1, 0:W], scalar1=1.0 / D), inc="ve")
            P.drain("vector")
            P.op("vector", lambda E, cs=cs, ce=ce: E.tensor_tensor(
                out=sc1[:, cs:ce], in0=mu[:, cs:ce], in1=mu[:, cs:ce], op=ALU.mult))
            P.drain("vector")
            P.op("vector", lambda E, cs=cs, ce=ce, W=W: E.scalar_tensor_tensor(
                out=rstd[:, cs:ce], in0=psum[1][0:1, 0:W], scalar=1.0 / D,
                in1=sc1[:, cs:ce], op0=ALU.mult, op1=ALU.subtract), inc="ve")
            P.wait("scalar", "ve")
            P.op("scalar", lambda E, cs=cs, ce=ce: E.activation(
                out=rstd[:, cs:ce], in_=rstd[:, cs:ce], func=AF.Sqrt,
                bias=self.sb_cst[0:1, 0:1]), inc="ae")
            P.wait("vector", "ae")
            P.op("vector", lambda E, cs=cs, ce=ce: E.reciprocal(
                out=rstd[:, cs:ce], in_=rstd[:, cs:ce]), inc="ve")
            P.wait("tensor", "ve")
            P.op("tensor", lambda E, cs=cs, ce=ce, W=W: E.matmul(
                psum[2][:, 0:W], self.sb_ones[0:1, :], mu[:, cs:ce],
                start=True, stop=True))
            P.op("tensor", lambda E, cs=cs, ce=ce, W=W: E.matmul(
                psum[3][:, 0:W], self.sb_ones[0:1, :], rstd[:, cs:ce],
                start=True, stop=True), inc="pe")
            P.wait("vector", "pe")
            for b in range(NB):
                xblk = xb[:, b * TOWN + cs:b * TOWN + ce]
                hblk = h2[:, b * TOWN + cs:b * TOWN + ce]
                P.op("vector", lambda E, x=xblk, hb=hblk, W=W: E.tensor_tensor(
                    out=hb, in0=x, in1=psum[2][:, 0:W], op=ALU.subtract))
            P.drain("vector")
            for b in range(NB):
                hblk = h2[:, b * TOWN + cs:b * TOWN + ce]
                P.op("vector", lambda E, hb=hblk, W=W: E.tensor_tensor(
                    out=hb, in0=hb, in1=psum[3][:, 0:W], op=ALU.mult))
            P.drain("vector")
            for b in range(NB):
                hblk = h2[:, b * TOWN + cs:b * TOWN + ce]
                P.op("vector", lambda E, hb=hblk, b=b: E.tensor_scalar(
                    out=hb, in0=hb, scalar1=cb1[:, b:b + 1], scalar2=sbv[:, b:b + 1],
                    op0=ALU.mult, op1=ALU.add),
                    inc=("ve" if b == NB - 1 else None))
            P.drain("vector")
            P.wait("tensor", "ve")
        P.wait("gpsimd", "ve")
        for b in range(NB):
            P.dma("gpsimd", self.sp_h2[b * 128:(b + 1) * 128, :],
                  h2[:, b * TOWN:(b + 1) * TOWN], "df2")
        P.wait("scalar", "ve")
        nsq = P.mark("pesq")
        for gi, (cs, ce) in enumerate(((0, 512), (512, 1024))):
            W = ce - cs
            P.wait("tensor", "ve")
            for b in range(NB):
                hblk = h2[:, b * TOWN + cs:b * TOWN + ce]
                nsq += 1
                if nsq > 2:
                    P.wait("scalar", "pesq", nsq - 2)
                P.op("scalar", lambda E, x=hblk, b=b, W=W: E.activation(
                    out=sq2[:, (b % 2) * 512:(b % 2) * 512 + W], in_=x,
                    func=AF.Square), inc="ae")
                P.wait("tensor", "ae")
                P.op("tensor", lambda E, b=b, W=W: E.matmul(
                    psum[1][0:1, 0:W], self.sb_ones[:, 0:1],
                    sq2[:, (b % 2) * 512:(b % 2) * 512 + W],
                    start=(b == 0), stop=(b == NB - 1)), inc="pesq")
            P.wait("vector", "pesq")
            P.op("vector", lambda E, cs=cs, ce=ce, W=W: E.tensor_scalar_mul(
                out=rstd[:, cs:ce], in0=psum[1][0:1, 0:W], scalar1=1.0 / D), inc="ve")
            P.wait("scalar", "ve")
            P.op("scalar", lambda E, cs=cs, ce=ce: E.activation(
                out=rstd[:, cs:ce], in_=rstd[:, cs:ce], func=AF.Sqrt,
                bias=self.sb_cst[0:1, 2:3]), inc="ae")
            P.wait("vector", "ae")
            P.op("vector", lambda E, cs=cs, ce=ce: E.reciprocal(
                out=rstd[:, cs:ce], in_=rstd[:, cs:ce]), inc="ve")
            P.wait("tensor", "ve")
            P.op("tensor", lambda E, gi=gi, cs=cs, ce=ce, W=W: E.matmul(
                psum[3 + gi][:, 0:W], self.sb_ones[0:1, :], rstd[:, cs:ce],
                start=True, stop=True), inc="pe")
        P.wait("vector", "pe")
        for half in range(2):
            cs = half * 512
            for b in range(NB):
                hblk = h2[:, b * TOWN + cs:b * TOWN + cs + 512]
                xblk = xb[:, b * TOWN + cs:b * TOWN + cs + 512]
                P.op("vector", lambda E, hb=hblk, xbk=xblk, half=half:
                     E.tensor_tensor(out=xbk, in0=hb, in1=psum[3 + half][:, 0:512],
                                     op=ALU.mult))
            P.drain("vector")
            for b in range(NB):
                xblk = xb[:, b * TOWN + cs:b * TOWN + cs + 512]
                hmb = hm_h[:, b * 512:(b + 1) * 512]
                lmb = hm_l[:, b * 512:(b + 1) * 512]
                tmp = sq2[:, 0:512]
                P.wait("vector", "ae", P.mark("ae"))
                P.wait("vector", "plor", P.mark("plor"))
                P.op("vector", lambda E, xbk=xblk, t=tmp, b=b: E.tensor_scalar(
                    out=t, in0=xbk, scalar1=ln2[:, b:b + 1], scalar2=None,
                    op0=ALU.mult), inc="ve")
                P.drain("vector")
                P.wait("scalar", "ve", P.mark("ve"))
                P.wait("scalar", "dh2", P.mark("dh2"))
                P.op("scalar", lambda E, t=tmp, o=hmb: E.activation(
                    out=o, in_=t, func=AF.Identity), inc="ae")
                P.wait("tensor", "ve", P.mark("ve"))
                P.op("tensor", lambda E, t=tmp, b=b, half=half: E.matmul(
                    psum[5][0:16, 0:512], alsb[:, b * AH:(b + 1) * AH], t,
                    start=(b == 0), stop=(b == NB - 1)), inc="plor")
                P.wait("vector", "ae", P.mark("ae"))
                P.op("vector", lambda E, t=tmp, h=hmb, o=lmb: E.tensor_tensor(
                    out=o, in0=t, in1=h, op=ALU.subtract), inc="ve")
                P.drain("vector")
                P.wait("gpsimd", "ve", P.mark("ve"))
                P.dma("gpsimd", self.sp_hmh[b * 128:(b + 1) * 128, cs:cs + 512],
                      hmb, "dh2")
                P.dma("gpsimd", self.sp_hml[b * 128:(b + 1) * 128, cs:cs + 512],
                      lmb, "dh2")
            P.wait("vector", "plor")
            P.wait("vector", "dlor", P.mark("dlor"))
            P.op("vector", lambda E: E.tensor_copy(out=lsp[0:16, 0:512],
                                                   in_=psum[5][0:16, 0:512]),
                 inc="ve")
            P.drain("vector")
            P.wait("gpsimd", "ve", P.mark("ve"))
            P.dma("gpsimd", self.sp_lor[0:16, cs:cs + 512], lsp[0:16, 0:512],
                  "dlor")
            P.wait("tensor", "ve", P.mark("ve"))
            for b in range(NB):
                xblk = xb[:, b * TOWN + cs:b * TOWN + cs + 512]
                P.op("vector", lambda E, xbk=xblk, b=b: E.tensor_scalar(
                    out=xbk, in0=xbk, scalar1=ln1[:, b:b + 1], scalar2=None,
                    op0=ALU.mult), inc=("ve" if b == NB - 1 else None))
            P.drain("vector")
        P.wait("gpsimd", "df2")
        P.wait("gpsimd", "dh2")
        P.wait("gpsimd", "dlor")
        P.barrier()
        AR.pop()
        xah = AR.alloc_bf16(NB * TOWN, "xah")
        xal = AR.alloc_bf16(NB * TOWN, "xal")
        xah3 = xah.rearrange("p (b t) -> p b t", b=NB)
        xal3 = xal.rearrange("p (b t) -> p b t", b=NB)
        for b in range(NB):
            xblk = xb[:, b * TOWN:(b + 1) * TOWN]
            self.split_pair(xblk, xah3[:, b, :], xal3[:, b, :], ve="ve", ae="ae")
        P.barrier()
        wsA = AR.alloc(2 * 512, "wsA3")
        wsH = AR.alloc_bf16(2 * 512, "wsH3")
        wsL = AR.alloc_bf16(2 * 512, "wsL3")
        ev = AR.alloc(2 * CH, "ev3")
        self.ws_bufs = (wsA, wsH, wsL)
        self._proj_fm_full(xah3, xal3, din["wq_a"], self.sp_qT, ev)
        self._proj_fm_full(xah3, xal3, din["wk_a"], self.kv_bounce, ev)
        self._proj_tok_full(xah3, xal3, din["wv_a"], ev)
        self._qk_norms()
        P.wait("gpsimd", "dxea")
        P.wait("gpsimd", "dxeb")
        P.wait("gpsimd", "dnrm")
        P.op("gpsimd", lambda E: E.collective_compute(
            "AllGather", ALU.bypass, replica_groups=[list(range(NCORE))],
            ins=[self.kv_bounce.ap()], outs=[self.kv_gat.ap()]), inc="cc")
        P.wait("gpsimd", "cc")
        P.barrier()
        AR.pop()

    def _proj_fm_full(self, xh3, xl3, wdram, dst, ev):
        P, psum = self.P, self.psum
        P.wait("tensor", "ve")
        P.wait("tensor", "ae")
        for q in range(4):
            for f in range(NB):
                wh, wl, slot = self._stream_w(
                    wdram, slice(f * 128, (f + 1) * 128),
                    slice(q * 512, q * 512 + 512), 512)
                st = (f == 0)
                sp = (f == NB - 1)
                for o in range(4):
                    whb = wh[:, o * 128:(o + 1) * 128]
                    wlb = wl[:, o * 128:(o + 1) * 128]
                    for th in range(2):
                        rh = xh3[:, f, th * 512:(th + 1) * 512]
                        rl = xl3[:, f, th * 512:(th + 1) * 512]
                        pb = psum[2 * o + th]
                        P.op("tensor", lambda E, w=whb, r=rh, pb=pb, st=st:
                             E.matmul(pb[:, :], w, r, start=st, stop=False))
                        P.op("tensor", lambda E, w=whb, r=rl, pb=pb: E.matmul(
                            pb[:, :], w, r, start=False, stop=False))
                        P.op("tensor", lambda E, w=wlb, r=rh, pb=pb, sp=sp:
                             E.matmul(pb[:, :], w, r, start=False, stop=sp),
                             inc="pwf")
                self.ws_marks[slot].append(P.mark("pwf"))
            P.wait("vector", "pwf")
            for o in range(4):
                og = q * 4 + o
                for th in range(2):
                    self._evac_store(psum[2 * o + th][:, :],
                                     slice(og * 128, (og + 1) * 128),
                                     slice(th * 512, (th + 1) * 512), dst, ev, None)
            P.wait("tensor", "vev")

    def _proj_tok_full(self, xh3, xl3, wdram, ev):
        P, psum = self.P, self.psum
        P.wait("tensor", "ve")
        P.wait("tensor", "ae")
        for osl in range(4):
            half, cloc = osl // 2, (osl % 2) * 512
            for f in range(NB):
                wh, wl, slot = self._stream_w(
                    wdram, slice(f * 128, (f + 1) * 128),
                    slice(osl * 512, (osl + 1) * 512), 512)
                st = (f == 0)
                sp = (f == NB - 1)
                for tb in range(8):
                    lh = xh3[:, f, tb * 128:(tb + 1) * 128]
                    ll = xl3[:, f, tb * 128:(tb + 1) * 128]
                    P.op("tensor", lambda E, l=lh, w=wh, tb=tb, st=st: E.matmul(
                        psum[tb][:, :], l, w, start=st, stop=False))
                    P.op("tensor", lambda E, l=ll, w=wh, tb=tb: E.matmul(
                        psum[tb][:, :], l, w, start=False, stop=False))
                    P.op("tensor", lambda E, l=lh, w=wl, tb=tb, sp=sp: E.matmul(
                        psum[tb][:, :], l, w, start=False, stop=sp), inc="pwf")
                self.ws_marks[slot].append(P.mark("pwf"))
            P.wait("vector", "pwf")
            for tb in range(8):
                r0 = 2048 + half * 1024 + tb * 128
                self._evac_store(psum[tb][:, :], slice(r0, r0 + 128),
                                 slice(cloc, cloc + 512), self.kv_bounce, ev, None)
            P.wait("tensor", "vev")

    def _qk_norms(self):
        P, AR, psum = self.P, self.AR, self.psum
        AR.push()
        qsq = AR.alloc(TOWN, "qsq")
        tmpf = AR.alloc(TOWN, "foldbuf")
        mx = AR.alloc(64, "nrmmax")
        P.wait("sync", "dxea")
        P.wait("sync", "dxeb")
        for which in range(2):
            for hd in range(AH):
                P.wait("sync", "pqn", P.mark("pqn"))
                if which == 0:
                    P.dma("sync", qsq[:, :],
                          self.sp_qT[hd * 128:(hd + 1) * 128, :], "dqn")
                else:
                    P.dma("sync", qsq[:, :],
                          self.kv_bounce[hd * 128:(hd + 1) * 128, 0:1024], "dqn")
                P.wait("vector", "dqn")
                P.wait("vector", "pqn", P.mark("pqn"))
                P.op("vector", lambda E: E.tensor_tensor(
                    out=qsq[:, :], in0=qsq[:, :], in1=qsq[:, :], op=ALU.mult),
                    inc="vqn")
                P.wait("tensor", "vqn")
                P.op("tensor", lambda E: E.matmul(
                    psum[6][0:1, 0:512], self.sb_ones[:, 0:1], qsq[:, 0:512],
                    start=True, stop=True))
                P.op("tensor", lambda E: E.matmul(
                    psum[7][0:1, 0:512], self.sb_ones[:, 0:1], qsq[:, 512:1024],
                    start=True, stop=True), inc="pqn")
                P.wait("vector", "pqn")
                P.op("vector", lambda E: E.tensor_copy(
                    out=tmpf[0:1, 512:1024], in_=psum[7][0:1, 0:512]))
                P.drain("vector")
                P.op("vector", lambda E: E.tensor_tensor(
                    out=tmpf[0:1, 0:512], in0=psum[6][0:1, 0:512],
                    in1=tmpf[0:1, 512:1024], op=ALU.max), inc="vqn")
                P.drain("vector")
                src, dst = 0, 512
                w = 256
                while w >= 1:
                    P.op("vector", lambda E, s=src, d=dst, w=w: E.tensor_tensor(
                        out=tmpf[0:1, d:d + w], in0=tmpf[0:1, s:s + w],
                        in1=tmpf[0:1, s + w:s + 2 * w], op=ALU.max))
                    P.drain("vector")
                    src, dst = dst, src
                    w //= 2
                P.op("vector", lambda E, s=src, hd=hd, which=which:
                     E.tensor_copy(out=mx[0:1, which * 16 + hd:which * 16 + hd + 1],
                                   in_=tmpf[0:1, s:s + 1]), inc="vqn")
                P.drain("vector")
        P.wait("gpsimd", "vqn")
        P.dma("gpsimd", self.kv_bounce[4096:4097, 0:32], mx[0:1, 0:32], "dnrm")
        AR.pop()

    def stage_attention(self):
        P, AR, din, psum = self.P, self.AR, self.din, self.psum
        GROW = 4097
        AR.push()
        qTh = AR.alloc_bf16(NB * TOWN, "qTh")
        qTl = AR.alloc_bf16(NB * TOWN, "qTl")
        qTh3 = qTh.rearrange("p (b t) -> p b t", b=NB)
        qTl3 = qTl.rearrange("p (b t) -> p b t", b=NB)
        qf = AR.alloc(TOWN, "qf")
        kfs = AR.alloc(512, "kfs")
        vfs = AR.alloc(512, "vfs")
        pfs = AR.alloc(4 * 512, "pfs")
        kbh = AR.alloc_bf16(512, "kbh")
        kbl = AR.alloc_bf16(512, "kbl")
        vbh = AR.alloc_bf16(512, "vbh")
        vbl = AR.alloc_bf16(512, "vbl")
        pbh = AR.alloc_bf16(4 * 512, "pbh")
        pbl = AR.alloc_bf16(4 * 512, "pbl")
        blend = AR.alloc(4 * 512, "blend")
        Ot = AR.alloc(512, "Ot")
        rec = AR.alloc(512, "recd")
        mrow = AR.alloc(64, "mrow")
        mcol = AR.alloc(16, "mcol")
        onesb = AR.alloc_bf16(8, "onesb")
        P.wait("sync", "dxea")
        P.wait("sync", "dxeb")
        for b in range(NB):
            P.wait("sync", "ve", P.mark("ve"))
            P.dma("sync", qf[:, :], self.sp_qT[b * 128:(b + 1) * 128, :], "dqa")
            P.wait("scalar", "dqa")
            P.wait("vector", "dqa")
            self.split_pair(qf, qTh3[:, b, :], qTl3[:, b, :], ve="ve", ae="ae")
        P.op("vector", lambda E: E.memset(onesb[:, 0:1], 1.0), inc="ve")
        for i in range(NCORE):
            P.dma("sync", blend[0:1, i * 16:(i + 1) * 16],
                  self.kv_gat.ap()[i * GROW + 4096:i * GROW + 4097, 16:32], "dqa")
        P.dma("sync", mrow[0:1, 16:32], self.kv_bounce[4096:4097, 0:16], "dqa")
        P.wait("vector", "dqa")
        P.op("vector", lambda E: E.tensor_copy(out=mrow[0:1, 0:16],
                                               in_=blend[0:1, 0:16]))
        P.drain("vector")
        for i in range(1, NCORE):
            P.op("vector", lambda E, i=i: E.tensor_tensor(
                out=mrow[0:1, 0:16], in0=mrow[0:1, 0:16],
                in1=blend[0:1, i * 16:(i + 1) * 16], op=ALU.max))
            P.drain("vector")
        P.op("vector", lambda E: E.memset(mrow[0:1, 32:64], 0.0), inc="ve")
        P.wait("scalar", "ve")
        P.op("scalar", lambda E: E.activation(out=mrow[0:1, 32:48],
                                              in_=mrow[0:1, 0:16], func=AF.Sqrt),
             inc="ae")
        P.op("scalar", lambda E: E.activation(out=mrow[0:1, 48:64],
                                              in_=mrow[0:1, 16:32], func=AF.Sqrt),
             inc="ae")
        P.wait("vector", "ae")
        P.op("vector", lambda E: E.tensor_tensor(
            out=mrow[0:1, 0:16], in0=mrow[0:1, 32:48], in1=mrow[0:1, 48:64],
            op=ALU.mult), inc="ve")
        P.drain("vector")
        P.op("vector", lambda E: E.tensor_scalar(
            out=mrow[0:1, 0:16], in0=mrow[0:1, 0:16],
            scalar1=-1.02 / float(np.sqrt(AHD)), scalar2=None, op0=ALU.mult),
            inc="ve")
        P.wait("tensor", "ve")
        P.op("tensor", lambda E: E.matmul(psum[6][:, 0:16], self.sb_ones[0:1, :],
                                          mrow[0:1, 0:16], start=True, stop=True),
             inc="pe")
        P.wait("vector", "pe")
        P.op("vector", lambda E: E.tensor_copy(out=mcol[:, 0:16],
                                               in_=psum[6][:, 0:16]), inc="ve")
        P.barrier()
        scale = 1.0 / float(np.sqrt(AHD))
        ps_o = psum[4]
        ps_d = psum[5]
        exp_mark = {0: 0, 1: 0, 2: 0, 3: 0}
        for qc in range(2):
            for hd in range(AH):
                P.wait("vector", "datt")
                for c in range(16):
                    ow = _owner(c)
                    li = 0 if c == ow else 1
                    tcol = (qc * 16 + c) * 2
                    for bq in range(4):
                        trl = self.sb_amasks[:, (2 + bq) * 512:(3 + bq) * 512]
                        P.op("vector", lambda E, t=trl, bq=bq, tcol=tcol:
                             E.tensor_scalar(
                                 out=blend[:, bq * 512:(bq + 1) * 512], in0=t,
                                 scalar1=self.sb_atab[:, tcol + 1:tcol + 2],
                                 scalar2=self.sb_atab[:, tcol:tcol + 1],
                                 op0=ALU.mult, op1=ALU.add))
                    P.drain("vector")
                    P.wait("sync", "aka", P.mark("aka"))
                    P.wait("sync", "vka", P.mark("vka"))
                    P.dma("sync", kfs[:, :],
                          self.kv_gat.ap()[ow * GROW + hd * 128:
                                           ow * GROW + (hd + 1) * 128,
                                           li * 512:(li + 1) * 512], "dka")
                    half = hd // 8
                    colb = (hd % 8) * 128
                    for tb in range(4):
                        r0 = ow * GROW + 2048 + half * 1024 + li * 512 + tb * 128
                        P.dma("sync", vfs[:, tb * 128:(tb + 1) * 128],
                              self.kv_gat.ap()[r0:r0 + 128, colb:colb + 128],
                              "dka")
                    P.wait("scalar", "dka")
                    P.wait("vector", "dka")
                    P.wait("scalar", "pS2", P.mark("pS2"))
                    P.wait("scalar", "pV", P.mark("pV"))
                    P.wait("vector", "pS2", P.mark("pS2"))
                    P.wait("vector", "pV", P.mark("pV"))
                    self.split_pair(kfs, kbh, kbl, ve="vka", ae="aka")
                    self.split_pair(vfs, vbh, vbl, ve="vka", ae="aka")
                    P.wait("tensor", "vka")
                    for ksb in range(4):
                        pS = psum[ksb]
                        if exp_mark[ksb]:
                            P.wait("tensor", "aE", exp_mark[ksb])
                        kh_s = kbh[:, ksb * 128:(ksb + 1) * 128]
                        kl_s = kbl[:, ksb * 128:(ksb + 1) * 128]
                        qh_s = qTh3[:, hd, qc * 512:(qc + 1) * 512]
                        ql_s = qTl3[:, hd, qc * 512:(qc + 1) * 512]
                        P.op("tensor", lambda E, a=kh_s, r=qh_s, pS=pS: E.matmul(
                            pS[:, :], a, r, start=True, stop=False))
                        P.op("tensor", lambda E, a=kh_s, r=ql_s, pS=pS: E.matmul(
                            pS[:, :], a, r, start=False, stop=False))
                        P.op("tensor", lambda E, a=kl_s, r=qh_s, pS=pS: E.matmul(
                            pS[:, :], a, r, start=False, stop=True), inc="pS2")
                    P.wait("scalar", "pS2")
                    P.wait("scalar", "vP", P.mark("vP"))
                    for ksb in range(4):
                        P.op("scalar", lambda E, ksb=ksb, hd=hd: E.activation(
                            out=pfs[:, ksb * 512:(ksb + 1) * 512],
                            in_=psum[ksb][:, :], func=AF.Exp,
                            bias=mcol[:, hd:hd + 1], scale=scale), inc="aE")
                        exp_mark[ksb] = P.mark("aE")
                    P.wait("vector", "aE")
                    for ksb in range(4):
                        P.op("vector", lambda E, ksb=ksb: E.tensor_tensor(
                            out=pfs[:, ksb * 512:(ksb + 1) * 512],
                            in0=pfs[:, ksb * 512:(ksb + 1) * 512],
                            in1=blend[:, ksb * 512:(ksb + 1) * 512], op=ALU.mult),
                            inc="vP")
                    P.drain("vector")
                    P.wait("scalar", "vP")
                    P.wait("scalar", "pV", P.mark("pV"))
                    P.wait("vector", "pV", P.mark("pV"))
                    for ksb in range(4):
                        self.split_pair(pfs[:, ksb * 512:(ksb + 1) * 512],
                                        pbh[:, ksb * 512:(ksb + 1) * 512],
                                        pbl[:, ksb * 512:(ksb + 1) * 512],
                                        ve="vP", ae="aP")
                    P.wait("tensor", "vP")
                    for ksb in range(4):
                        vh_s = vbh[:, ksb * 128:(ksb + 1) * 128]
                        vl_s = vbl[:, ksb * 128:(ksb + 1) * 128]
                        ph_s = pbh[:, ksb * 512:(ksb + 1) * 512]
                        pl_s = pbl[:, ksb * 512:(ksb + 1) * 512]
                        stt_ = (c == 0 and ksb == 0)
                        fin_ = (c == 15 and ksb == 3)
                        P.op("tensor", lambda E, a=vh_s, r=ph_s, st=stt_:
                             E.matmul(ps_o[:, :], a, r, start=st, stop=False))
                        P.op("tensor", lambda E, a=vh_s, r=pl_s: E.matmul(
                            ps_o[:, :], a, r, start=False, stop=False))
                        P.op("tensor", lambda E, a=vl_s, r=ph_s, fin=fin_:
                             E.matmul(ps_o[:, :], a, r, start=False, stop=fin))
                        P.op("tensor", lambda E, r=ph_s, st=stt_: E.matmul(
                            ps_d[0:1, :], onesb[:, 0:1], r, start=st, stop=False))
                        P.op("tensor", lambda E, r=pl_s, fin=fin_: E.matmul(
                            ps_d[0:1, :], onesb[:, 0:1], r, start=False, stop=fin),
                            inc="pV")
                P.wait("vector", "pV")
                P.op("vector", lambda E: E.reciprocal(out=rec[0:1, 0:512],
                                                      in_=ps_d[0:1, :]), inc="vF")
                P.wait("tensor", "vF")
                P.op("tensor", lambda E: E.matmul(
                    psum[6][:, :], self.sb_ones[0:1, :], rec[0:1, 0:512],
                    start=True, stop=True), inc="pF")
                P.wait("vector", "pF")
                P.op("vector", lambda E: E.tensor_copy(out=blend[:, 0:512],
                                                       in_=psum[6][:, :]))
                P.drain("vector")
                P.op("vector", lambda E: E.tensor_tensor(
                    out=Ot[:, :], in0=ps_o[:, :], in1=blend[:, 0:512],
                    op=ALU.mult), inc="vF")
                P.drain("vector")
                P.wait("gpsimd", "vF")
                P.dma("gpsimd", self.sp_att[hd * 128:(hd + 1) * 128,
                                            qc * 512:(qc + 1) * 512], Ot[:, :],
                      "datt")
                P.barrier()
        AR.pop()
        self.tap("spatt", lambda: self.sp_att.ap(), [D, TOWN])

    def stage_attwo(self):
        P, AR, din = self.P, self.AR, self.din
        AR.push()
        ah = AR.alloc_bf16(NB * TOWN, "awh")
        al = AR.alloc_bf16(NB * TOWN, "awl")
        ah3 = ah.rearrange("p (b t) -> p b t", b=NB)
        al3 = al.rearrange("p (b t) -> p b t", b=NB)
        af = AR.alloc(TOWN, "awf")
        P.wait("sync", "datt")
        for b in range(NB):
            P.wait("sync", "ve", P.mark("ve"))
            P.dma("sync", af[:, :], self.sp_att[b * 128:(b + 1) * 128, :], "dwo2")
            P.wait("scalar", "dwo2")
            P.wait("vector", "dwo2")
            self.split_pair(af, ah3[:, b, :], al3[:, b, :], ve="ve", ae="ae")
        P.barrier()
        wsA = AR.alloc(2 * 512, "wsA5")
        wsH = AR.alloc_bf16(2 * 512, "wsH5")
        wsL = AR.alloc_bf16(2 * 512, "wsL5")
        ev = AR.alloc(2 * CH, "ev5")
        self.ws_bufs = (wsA, wsH, wsL)
        self._proj_fm_full(ah3, al3, din["wo_a"], self.sp_y, ev)
        P.barrier()
        AR.pop()
        self.tap("spy", lambda: self.sp_y.ap(), [D, TOWN])

    def stage_mlp(self):
        P, AR, din, psum = self.P, self.AR, self.din, self.psum
        AR.push()
        hm_h = AR.alloc_bf16(NB * 512, "mhm")
        hm_l = AR.alloc_bf16(NB * 512, "mhl")
        prod_h = AR.alloc_bf16(FB * 512, "prodh")
        prod_l = AR.alloc_bf16(FB * 512, "prodl")
        alsb = AR.alloc(NB * AH, "alsb4")
        lorb = AR.alloc(512, "lorb")
        lorm = AR.alloc(512, "lorm")
        modr = AR.alloc(512, "modr")
        gat = AR.alloc(512, "gate")
        upf = AR.alloc(512, "upf")
        bls = AR.alloc(2 * 128, "bls")
        wsA = AR.alloc(2 * 512, "wsA4")
        wsH = AR.alloc_bf16(2 * 512, "wsH4")
        wsL = AR.alloc_bf16(2 * 512, "wsL4")
        xrb = AR.alloc(512, "xrb4")
        hrb = AR.alloc(512, "hrb4")
        self.ws_bufs = (wsA, wsH, wsL)
        self.ws_nslots = 8
        self.ws_stride = 128
        gb = self.modv(5)
        for b in range(NB):
            P.dma("sync", alsb[:, b * AH:(b + 1) * AH],
                  din["a_lora"][b * 128:(b + 1) * 128, :], "dml")
        P.wait("sync", "dxea")
        P.wait("sync", "dxeb")
        P.wait("sync", "dlor")
        bl_mark = {0: 0, 1: 0}
        for ch in range(2):
            P.barrier()
            cs = ch * 512
            for b in range(NB):
                P.dma("sync", hm_h[:, b * 512:(b + 1) * 512],
                      self.sp_hmh[b * 128:(b + 1) * 128, cs:cs + 512], "dml")
                P.dma("sync", hm_l[:, b * 512:(b + 1) * 512],
                      self.sp_hml[b * 128:(b + 1) * 128, cs:cs + 512], "dml")
            P.dma("sync", lorb[0:16, 0:512], self.sp_lor[0:16, cs:cs + 512], "dml")
            P.wait("tensor", "dml")
            for b in range(NB):
                P.wait("sync", "pma", P.mark("pma"))
                P.dma("sync", gat[:, :],
                      self.sp_y[b * 128:(b + 1) * 128, cs:cs + 512], "dma2")
                P.wait("tensor", "dma2")
                P.op("tensor", lambda E, b=b: E.matmul(
                    psum[7][0:16, 0:512], alsb[:, b * AH:(b + 1) * AH], gat[:, :],
                    start=(b == 0), stop=(b == NB - 1)), inc="pma")
            P.wait("scalar", "pma")
            P.op("scalar", lambda E: E.activation(out=modr[0:16, 0:512],
                                                  in_=psum[7][0:16, 0:512],
                                                  func=AF.Tanh), inc="ae")
            P.wait("vector", "ae")
            P.op("vector", lambda E: E.tensor_scalar(
                out=modr[0:16, 0:512], in0=modr[0:16, 0:512], scalar1=MOD_SCALE,
                scalar2=None, op0=ALU.mult), inc="ve")
            P.drain("vector")
            P.wait("vector", "dml")
            P.op("vector", lambda E: E.tensor_tensor(
                out=lorm[0:16, 0:512], in0=lorb[0:16, 0:512],
                in1=modr[0:16, 0:512], op=ALU.mult), inc="ve")
            P.drain("vector")
            P.wait("tensor", "ve")
            for fb in range(FB):
                slot = fb % 2
                if bl_mark[slot]:
                    P.wait("sync", "pBt", bl_mark[slot])
                P.dma("sync", bls[0:16, slot * 128:(slot + 1) * 128],
                      din["b_lora"][0:16, fb * 128:(fb + 1) * 128], "dbl")
                for which, wname, bank in ((0, "w_in", 2), (1, "w_up", 3)):
                    for f in range(NB):
                        wh, wl, slot2 = self._stream_w(
                            din[wname], slice(f * 128, (f + 1) * 128),
                            slice(fb * 128, (fb + 1) * 128), 128)
                        rh = hm_h[:, f * 512:(f + 1) * 512]
                        rl = hm_l[:, f * 512:(f + 1) * 512]
                        P.op("tensor", lambda E, w=wh, r=rh, bank=bank, f=f:
                             E.matmul(psum[bank][:, :], w[:, 0:128], r,
                                      start=(f == 0), stop=False))
                        P.op("tensor", lambda E, w=wh, r=rl, bank=bank: E.matmul(
                            psum[bank][:, :], w[:, 0:128], r, start=False,
                            stop=False))
                        P.op("tensor", lambda E, w=wl, r=rh, bank=bank, f=f:
                             E.matmul(psum[bank][:, :], w[:, 0:128], r,
                                      start=False, stop=(f == NB - 1)), inc="pwf")
                        self.ws_marks[slot2].append(P.mark("pwf"))
                    if which == 0:
                        P.wait("tensor", "dbl")
                        P.op("tensor", lambda E, slot=slot: E.matmul(
                            psum[6][0:128, 0:512],
                            bls[0:16, slot * 128:(slot + 1) * 128],
                            lorm[0:16, 0:512], start=True, stop=True), inc="pBt")
                        bl_mark[slot] = P.mark("pBt")
                P.wait("vector", "pwf")
                P.wait("vector", "pBt")
                P.wait("vector", "ae", P.mark("ae"))
                P.op("vector", lambda E: E.tensor_copy(
                    out=xrb[:, :], in_=psum[6][0:128, 0:512]))
                P.drain("vector")
                P.op("vector", lambda E: E.tensor_tensor(
                    out=hrb[:, :], in0=psum[2][:, :], in1=xrb[:, :],
                    op=ALU.add), inc="ve")
                P.drain("vector")
                P.wait("scalar", "ve", P.mark("ve"))
                P.op("scalar", lambda E: E.activation(out=gat[:, :], in_=hrb[:, :],
                                                      func=AF.Silu), inc="ae")
                P.wait("vector", "ae")
                P.op("vector", lambda E: E.tensor_tensor(
                    out=upf[:, :], in0=psum[3][:, :], in1=gat[:, :], op=ALU.mult),
                    inc="ve")
                P.drain("vector")
                P.wait("scalar", "ve", P.mark("ve"))
                P.op("scalar", lambda E, fb=fb: E.activation(
                    out=prod_h[:, fb * 512:(fb + 1) * 512], in_=upf[:, :],
                    func=AF.Identity), inc="ae")
                P.wait("vector", "ae")
                P.op("vector", lambda E, fb=fb: E.tensor_tensor(
                    out=prod_l[:, fb * 512:(fb + 1) * 512], in0=upf[:, :],
                    in1=prod_h[:, fb * 512:(fb + 1) * 512], op=ALU.subtract),
                    inc="ve")
                P.drain("vector")
                P.wait("tensor", "ve")
            for ob in range(NB):
                for fb in range(FB):
                    wh, wl, slot2 = self._stream_w(
                        din["w_down"], slice(fb * 128, (fb + 1) * 128),
                        slice(ob * 128, (ob + 1) * 128), 128)
                    ph = prod_h[:, fb * 512:(fb + 1) * 512]
                    pl = prod_l[:, fb * 512:(fb + 1) * 512]
                    bank = ob % 2
                    P.op("tensor", lambda E, w=wh, r=ph, bank=bank, fb=fb:
                         E.matmul(psum[bank][:, :], w[:, 0:128], r,
                                  start=(fb == 0), stop=False))
                    P.op("tensor", lambda E, w=wh, r=pl, bank=bank: E.matmul(
                        psum[bank][:, :], w[:, 0:128], r, start=False, stop=False))
                    P.op("tensor", lambda E, w=wl, r=ph, bank=bank, fb=fb:
                         E.matmul(psum[bank][:, :], w[:, 0:128], r, start=False,
                                  stop=(fb == FB - 1)), inc="pwf")
                    self.ws_marks[slot2].append(P.mark("pwf"))
                P.wait("sync", "ve", P.mark("ve"))
                P.wait("sync", "ae", P.mark("ae"))
                P.wait("sync", "dout")
                P.dma("sync", xrb[:, :],
                      self.sp_x1[ob * 128:(ob + 1) * 128, cs:cs + 512], "dmx")
                P.dma("sync", hrb[:, :],
                      self.sp_h2[ob * 128:(ob + 1) * 128, cs:cs + 512], "dmx")
                P.wait("vector", "dmx")
                P.wait("vector", "pwf")
                P.op("vector", lambda E, bank=ob % 2: E.tensor_tensor(
                    out=hrb[:, :], in0=hrb[:, :], in1=psum[bank][:, :],
                    op=ALU.add))
                P.drain("vector")
                P.op("vector", lambda E, ob=ob: E.scalar_tensor_tensor(
                    out=xrb[:, :], in0=hrb[:, :], scalar=gb[:, ob:ob + 1],
                    in1=xrb[:, :], op0=ALU.mult, op1=ALU.add), inc="ve")
                P.drain("vector")
                P.wait("gpsimd", "ve", P.mark("ve"))
                P.dma("gpsimd", self.out_xT[ob * 128:(ob + 1) * 128, cs:cs + 512],
                      xrb[:, :], "dout")
                P.wait("tensor", "ve")
        P.wait("gpsimd", "dout")
        AR.pop()
        P.barrier()

    def emit_tail(self):
        P = self.P
        P.barrier()
        if self.wkv_parts < 4:
            P.dma("gpsimd", self.out_xT[:, :], self.sp_h[:, 0:TOWN], "dout")
        P.wait("gpsimd", "dout")


def host_prepare(inputs):
    f32 = np.float32
    x = np.asarray(inputs["L0_attn_x"], f32)[0]
    xT = np.ascontiguousarray(x.T)

    def fm(v):
        return np.ascontiguousarray(np.asarray(v, f32).reshape(NB, 128).T)

    vec_list = [
        fm(inputs["L0_attn_time_maa_x"].reshape(-1)),
        fm(inputs["L0_attn_time_maa_w"].reshape(-1)),
        fm(inputs["L0_attn_time_maa_k"].reshape(-1)),
        fm(inputs["L0_attn_time_maa_v"].reshape(-1)),
        fm(inputs["L0_attn_time_maa_r"].reshape(-1)),
        fm(inputs["L0_attn_time_maa_g"].reshape(-1)),
        fm(inputs["L0_attn_time_decay"].reshape(-1)),
        fm(inputs["L0_attn_time_faaaa"].reshape(-1)),
        fm(inputs["L0_attn_ln_gamma"].reshape(-1)),
        fm(inputs["L0_attn_ln_beta"].reshape(-1)),
        fm(inputs["L0_ffn_ln1_w"].reshape(-1)),
        fm(inputs["L0_ffn_ln2_w"].reshape(-1)),
        fm(inputs["L0_c"].reshape(-1)),
    ]
    vecs = np.ascontiguousarray(np.concatenate(vec_list, axis=1))

    def tp(w):
        return np.ascontiguousarray(np.asarray(w, f32).T)

    common = {
        "wr": tp(inputs["L0_attn_W_r"]), "wk": tp(inputs["L0_attn_W_k"]),
        "wv": tp(inputs["L0_attn_W_v"]), "wg": tp(inputs["L0_attn_W_g"]),
        "wo": tp(inputs["L0_attn_W_o"]),
        "wq_a": tp(inputs["L0_ffn_Wq"]), "wk_a": tp(inputs["L0_ffn_Wk"]),
        "wv_a": tp(inputs["L0_ffn_Wv"]), "wo_a": tp(inputs["L0_ffn_Wo"]),
        "w_in": tp(inputs["L0_ffn_W_in_base"]), "w_up": tp(inputs["L0_ffn_W_up_base"]),
        "w_down": tp(inputs["L0_ffn_W_down"]),
        "maa_w1": np.ascontiguousarray(np.asarray(inputs["L0_attn_time_maa_w1"], f32)),
        "maa_w2": np.ascontiguousarray(
            np.asarray(inputs["L0_attn_time_maa_w2"], f32).reshape(5 * DMIX, D)),
        "dec_w1": np.ascontiguousarray(np.asarray(inputs["L0_attn_time_decay_w1"], f32)),
        "dec_w2": np.ascontiguousarray(np.asarray(inputs["L0_attn_time_decay_w2"], f32)),
        "a_lora": np.ascontiguousarray(np.asarray(inputs["L0_ffn_A"], f32)),
        "b_lora": np.ascontiguousarray(np.asarray(inputs["L0_ffn_B"], f32)),
        "vecs": vecs,
        "ident": np.eye(128, dtype=f32),
        "wkv_mask": (np.arange(128)[:, None] < np.arange(128)[None, :]).astype(f32),
        "ind2": np.ascontiguousarray(
            np.stack([(np.arange(128) < 64), (np.arange(128) >= 64)], 1).astype(f32)),
        "ind2t": np.ascontiguousarray(
            np.stack([(np.arange(128) < 64), (np.arange(128) >= 64)], 0).astype(f32)),
        "att_masks": _att_masks(),
    }
    wmod = np.asarray(inputs["L0_wmod"], f32)

    in_maps = []
    for i in range(NCORE):
        lo, hi = i, 15 - i
        slab = np.zeros((D, SLAB), f32)
        slab[:, 0:512] = xT[:, lo * CH:(lo + 1) * CH]
        slab[:, 512:1024] = xT[:, hi * CH:(hi + 1) * CH]
        if lo > 0:
            slab[:, 1024] = xT[:, lo * CH - 1]
        slab[:, 1025] = xT[:, hi * CH - 1]
        bnd = np.ones((128, 2), f32)
        if lo == 0:
            bnd[:, 0] = 0.0
        m = dict(common)
        m["xslab"] = slab
        m["bndflag"] = bnd
        m["wmod_sl"] = np.ascontiguousarray(
            wmod[:, i * (6 * D // NCORE):(i + 1) * (6 * D // NCORE)])
        m["sel_wkv"] = _sel_wkv(i)
        m["att_qsel"] = _att_qsel(i)
        m["att_sched"] = _att_sched(i)
        m["att_tab"] = _att_tab(i)
        in_maps.append(m)
    return in_maps


def _att_masks():
    out = np.zeros((128, 6 * 512), np.float32)
    out[:, 0:512] = 1.0
    s = np.arange(128)[:, None]
    t = np.arange(512)[None, :]
    for b in range(4):
        out[:, (2 + b) * 512:(3 + b) * 512] = (128 * b + s <= t).astype(np.float32)
    return out


def _sel_wkv(i):
    out = np.zeros((128, 64), np.float32)
    sel_lo = np.zeros(15, np.float32)
    sel_hi = np.zeros(15, np.float32)
    if i - 1 >= 0:
        sel_lo[i - 1] = 1.0
    sel_hi[14 - i] = 1.0
    out[:, 0:15] = sel_lo
    out[:, 16:31] = 1.0 - sel_lo
    out[:, 32:47] = sel_hi
    out[:, 48:63] = 1.0 - sel_hi
    return out


def _att_qsel(i):
    GL = (i + 1 + 1) // 2
    out = np.zeros((128, 18), np.float32)
    for j in range(9):
        if j < GL:
            out[:, 2 * j] = 1.0
        else:
            out[:, 2 * j + 1] = 1.0
    return out


def _att_tab(i):
    out = np.zeros((128, 64), np.float32)
    for qc in range(2):
        qg = i if qc == 0 else 15 - i
        for c in range(16):
            act = 1.0 if c <= qg else 0.0
            dg = 1.0 if c == qg else 0.0
            out[:, (qc * 16 + c) * 2] = act * (1.0 - dg)
            out[:, (qc * 16 + c) * 2 + 1] = act * dg
    return out


def _att_sched(i):
    return np.zeros((1, 128), np.int32)


_NC_CACHE = {}


def get_nc(dbg=(), wkv_parts=4):
    key = (tuple(sorted(dbg)), wkv_parts)
    if key not in _NC_CACHE:
        b = Builder(dbg, wkv_parts)
        nc = b.build()
        nc.compile()
        _NC_CACHE[key] = (nc, b)
    return _NC_CACHE[key]


def kernel(**inputs):
    in_maps = host_prepare(inputs)
    nc, b = get_nc()
    res = run_bass_kernel_spmd(nc, in_maps, core_ids=list(range(NCORE)))
    out = np.empty((T, D), np.float32)
    for i, r in enumerate(res.results):
        o = r["out_xT"]
        out[i * CH:(i + 1) * CH] = o[:, 0:512].T
        out[(15 - i) * CH:(16 - i) * CH] = o[:, 512:1024].T
    return out.reshape(1, T, D)
